# version 17
# baseline (speedup 1.0000x reference)
import math
import numpy as np
import ml_dtypes
import concourse.bass as bass
import concourse.mybir as mybir
from concourse.bass_utils import run_bass_kernel_spmd

F32 = mybir.dt.float32
BF16 = mybir.dt.bfloat16
F16 = mybir.dt.float16
I32 = mybir.dt.int32
AF = mybir.ActivationFunctionType
ALU = mybir.AluOpType
AX = mybir.AxisListType.X

NCORES = 8
D = 2048
FD = 16
QCD = 12
KVCD = 4
HN = 16
FF = 8192
NK = 4608
NKC = 36
NQ = 1024
NQT = 8
HPG = 2
NGRP = HN // HPG
SCALE = 1.0 / math.sqrt(128.0)
CB = 256.0
F16B = 1536.0
EPS = 1e-5
EXPB = -5.0
KB = 1024
SB0 = 16512
NW = float(D) * FF
DEBUG = False
STOP_AFTER = None


def build_nc():
    nc = bass.Bass("TRN2", target_bir_lowering=False)

    xk_d = nc.dram_tensor("xk", [NK, D], F32, kind="ExternalInput").ap()
    xo_d = nc.dram_tensor("xo", [NQ, D], F32, kind="ExternalInput").ap()
    wdq_d = nc.dram_tensor("wdqT", [D, 1536], F32, kind="ExternalInput").ap()
    wuq_d = nc.dram_tensor("wuqT", [1536, D], F32, kind="ExternalInput").ap()
    wdkv_d = nc.dram_tensor("wdkvT", [D, 512], F32, kind="ExternalInput").ap()
    wukv_d = nc.dram_tensor("wukvT", [512, 2 * D], F32, kind="ExternalInput").ap()
    wout_d = nc.dram_tensor("woutT", [D, D], F32, kind="ExternalInput").ap()
    wg_d = nc.dram_tensor("wgT", [D, FF], F32, kind="ExternalInput").ap()
    wu_d = nc.dram_tensor("wuT", [D, FF], F32, kind="ExternalInput").ap()
    wd_d = nc.dram_tensor("wdT", [FF, D], F32, kind="ExternalInput").ap()
    masks_d = nc.dram_tensor("masks", [512, 512], BF16, kind="ExternalInput").ap()
    ones_d = nc.dram_tensor("onesc", [128, 128], BF16, kind="ExternalInput").ap()
    ident_d = nc.dram_tensor("identc", [128, 128], BF16, kind="ExternalInput").ap()
    y_d = nc.dram_tensor("y", [NQ, D], F32, kind="ExternalOutput").ap()
    hs_d = nc.dram_tensor("hs", [NQ, FF], F32).ap()
    aos_d = nc.dram_tensor("aos", [FD, 128, NQ], BF16).ap()
    xqs_d = nc.dram_tensor("xqs", [FD, 128, NQ], BF16).ap()

    AT = nc.alloc_sbuf_tensor_at
    identb = AT("identb", [128, 128], BF16, offset=SB0 + 0)
    onesb = AT("onesb", [128, 128], BF16, offset=SB0 + 256)
    masksb = AT("masksb", [128, 4, 512], BF16, offset=SB0 + 512)
    off = SB0 + 4608
    def nx(shape, dt, name):
        nonlocal off
        t = AT(name, shape, dt, offset=off)
        sz = 1
        for s in shape[1:]:
            sz *= s
        off += sz * (4 if dt in (F32, I32) else 2)
        off = (off + 31) & ~31
        return t
    sums_c = nx([128, 48], F32, "sums_c")
    sumsq_c = nx([128, 48], F32, "sumsq_c")
    lnt0 = nx([128, 48], F32, "lnt0")
    lnt1 = nx([128, 48], F32, "lnt1")
    lnt2 = nx([128, 48], F32, "lnt2")
    scq = nx([128, 8], F32, "scq")
    xnmax = nx([128, 8], F32, "xnmax")
    acc3 = nx([128, 4], F32, "acc3")
    accT = nx([128, 4], F32, "accT")
    thr = nx([128, 16], F32, "thr")
    sgs = nx([128, 8], F32, "sgs")
    hmax = nx([128, 8], F32, "hmax")
    dscl = nx([128, 8], F32, "dscl")
    hmaxa = nx([128, FD, 8], F32, "hmaxa")
    auxs = nx([128, 16], F32, "auxs")
    cpool = nx([128, 4], F32, "cpool")
    onesf = nx([128, 128], F32, "onesf")
    identf = nx([128, 128], F32, "identf")
    cids0 = nx([1, 16], I32, "cids0")
    assert off <= SB0 + 8 * KB, off
    nc.const_aps.register(F32, EPS, cpool.ap()[:, 0:1])
    nc.const_aps.register(F32, CB, cpool.ap()[:, 1:2])
    nc.const_aps.register(F32, EXPB, cpool.ap()[:, 2:3])
    nc.const_aps.register(F32, F16B, cpool.ap()[:, 3:4])

    ckv = AT("ckv", [128, KVCD, NK], BF16, offset=SB0 + 8 * KB)
    wukv = AT("wukv", [128, KVCD, 2 * D], BF16, offset=SB0 + 44 * KB)
    qfm = AT("qfm", [128, HN, NQ], BF16, offset=SB0 + 76 * KB)
    otall = AT("otall", [128, HN, NQ], BF16, offset=SB0 + 108 * KB)
    wdkv = AT("wdkv", [128, FD, 512], BF16, offset=SB0 + 108 * KB)
    xntb = AT("xntb", [128, 2, D], BF16, offset=SB0 + 140 * KB)
    xnf = AT("xnf", [128, 2, FD, 512], BF16, offset=SB0 + 148 * KB)
    statda = AT("statda", [128, D], F32, offset=SB0 + 124 * KB)
    xt0 = AT("xt0", [128, D], F32, offset=SB0 + 188 * KB)
    xt1 = AT("xt1", [128, D], F32, offset=SB0 + 196 * KB)
    xqt = AT("xqt", [128, NQT, D], BF16, offset=SB0 + 76 * KB)
    xnofm = AT("xnofm", [128, FD, NQ], BF16, offset=SB0 + 108 * KB)
    xnof = AT("xnof", [128, 2, D], F32, offset=SB0 + 148 * KB)
    qu_u = AT("qu_u", [128, D], F32, offset=SB0 + 164 * KB)
    qu_h = AT("qu_h", [128, D], F16, offset=SB0 + 172 * KB)
    qu_g = AT("qu_g", [128, D], F32, offset=SB0 + 180 * KB)
    wst_f = AT("wst_f", [128, 2, D], F32, offset=SB0 + 148 * KB)
    wst_b = AT("wst_b", [128, 2, FD, 128], BF16, offset=SB0 + 164 * KB)
    qcfm = AT("qcfm", [128, QCD, NQ], BF16, offset=SB0 + 172 * KB)
    kg = AT("kg", [128, HPG, NK], BF16, offset=SB0 + 140 * KB)
    vg = AT("vg", [128, NKC, HPG * 128], BF16, offset=SB0 + 158 * KB)
    ptile = AT("ptile", [128, 2, HPG, 512], BF16, offset=SB0 + 176 * KB)
    recip_s = AT("recip_s", [128, 1024], F32, offset=SB0 + 180 * KB)
    rep_s = AT("rep_s", [128, 2, 512], F32, offset=SB0 + 184 * KB)
    aostg = AT("aostg", [128, 2, NQ], BF16, offset=SB0 + 148 * KB)
    wof = AT("wof", [128, 2, D], F32, offset=SB0 + 188 * KB)
    wob = AT("wob", [128, 2, FD, 128], BF16, offset=SB0 + 140 * KB)
    gsq = AT("gsq", [128, 2, 4, 512], F32, offset=SB0 + 8 * KB)
    usq = AT("usq", [128, 2, 4, 512], F32, offset=SB0 + 24 * KB)
    tgq2 = AT("tgq2", [128, FD, 512], BF16, offset=SB0 + 40 * KB)
    tuq2 = AT("tuq2", [128, FD, 512], BF16, offset=SB0 + 56 * KB)
    tgq = AT("tgq", [128, FD, 512], BF16, offset=SB0 + 72 * KB)
    tuq = AT("tuq", [128, FD, 512], BF16, offset=SB0 + 88 * KB)
    xqfm = AT("xqfm", [128, FD, NQ], BF16, offset=SB0 + 104 * KB)
    hg = AT("hg", [128, NQT, 512], F32, offset=SB0 + 136 * KB)
    hstg = AT("hstg", [128, NQT, 512], F32, offset=SB0 + 152 * KB)
    cmp_a = AT("cmp_a", [128, 4, 512], BF16, offset=SB0 + 200 * KB)
    hqfm = AT("hqfm", [128, 64, NQ], BF16, offset=SB0 + 40 * KB)
    hbuf = AT("hbuf", [128, FF], F32, offset=SB0 + 168 * KB)
    q2_h = AT("q2_h", [128, D], F16, offset=SB0 + 8 * KB)
    q2_u = AT("q2_u", [128, D], F32, offset=SB0 + 28 * KB)
    hqtm = AT("hqtm", [128, D], BF16, offset=SB0 + 24 * KB)
    ffnb = AT("ffnb", [128, FD, NQ], BF16, offset=SB0 + 8 * KB)
    dslab = AT("dslab", [128, FD, 128], F32, offset=SB0 + 168 * KB)
    dcma = AT("dcma", [128, FD, 128], BF16, offset=SB0 + 176 * KB)
    tdq = AT("tdq", [128, 64, 128], BF16, offset=SB0 + 184 * KB)
    pa0 = AT("pa0", [128, D], F32, offset=SB0 + 76 * KB)
    pa1 = AT("pa1", [128, D], F32, offset=SB0 + 84 * KB)
    wdst = AT("wdst", [128, FD, 256], F32, offset=SB0 + 92 * KB)
    pc0 = AT("pc0", [128, D], F32, offset=SB0 + 180 * KB)
    pd0 = AT("pd0", [128, D], F32, offset=SB0 + 196 * KB)
    pe0 = AT("pe0", [128, D], F32, offset=SB0 + 108 * KB)
    pe1 = AT("pe1", [128, D], F32, offset=SB0 + 116 * KB)
    pt0 = AT("pt0", [128, D], F32, offset=SB0 + 158 * KB)
    pt1 = AT("pt1", [128, D], F32, offset=SB0 + 166 * KB)
    accTw = AT("accTw", [128, 3, 64], F32, offset=SB0 + 204 * KB)
    aofm = AT("aofm", [128, FD, NQ], BF16, offset=SB0 + 40 * KB)
    sumb = AT("sumb", [128, FD, NQ], BF16, offset=SB0 + 72 * KB)
    xtb = AT("xtb", [128, 2, D], F32, offset=SB0 + 104 * KB)
    ystg = AT("ystg", [128, 2, D], F32, offset=SB0 + 120 * KB)

    psb0 = nc.alloc_psum_tensor("psb0", [128, 512], F32)
    psb1 = nc.alloc_psum_tensor("psb1", [128, 512], F32)
    psp23 = nc.alloc_psum_tensor("psp23", [128, 1024], F32)
    psp45 = nc.alloc_psum_tensor("psp45", [128, 1024], F32)
    psb6 = nc.alloc_psum_tensor("psb6", [128, 512], F32)
    psb7 = nc.alloc_psum_tensor("psb7", [128, 512], F32)

    def bank(i):
        if i == 0:
            return psb0[:]
        if i == 1:
            return psb1[:]
        if i in (2, 3):
            return psp23[:, (i - 2) * 512:(i - 2) * 512 + 512]
        if i in (4, 5):
            return psp45[:, (i - 4) * 512:(i - 4) * 512 + 512]
        if i == 6:
            return psb6[:]
        return psb7[:]

    def bank2(i):
        return psp23[:] if i == 2 else psp45[:]

    cid_d = nc.dram_tensor("cidc", [1, 4], I32, kind="ExternalInput").ap()
    cids = cids0

    sem_cms, sems = [], {}
    for s in ("c0", "x0", "x1", "w0", "w1", "g0", "g1", "g2", "p0", "p1",
              "mT", "mV", "mS", "mG", "mY"):
        cm = nc.semaphore(s)
        sems[s] = cm.__enter__()
        sem_cms.append(cm)

    class _Dummy:
        def then_inc(self, *a, **k):
            return self

        def _wait_ge(self, *a, **k):
            return self

    class _NullGuard:
        def __enter__(self):
            return self

        def __exit__(self, *a):
            return False

    _DUM = _Dummy()
    _NG = _NullGuard()

    class EP:
        def __init__(self, real, on):
            self._r = real
            self._on = on

        def __getattr__(self, name):
            if self._on:
                return getattr(self._r, name)
            if name in ("If_eq", "If_ne", "If_lt"):
                return lambda *a, **k: _NG
            return lambda *a, **k: _DUM

    DMA_SEMS = ("c0", "x0", "x1", "w0", "w1", "g0", "g1", "g2", "p0", "p1")

    def emit(nc, ACTIVE, cnt):
        real = {"tensor": nc.tensor, "vector": nc.vector, "scalar": nc.scalar,
                "gpsimd": nc.gpsimd, "sync": nc.sync}
        T = EP(nc.tensor, ACTIVE == "tensor")
        V = EP(nc.vector, ACTIVE == "vector")
        S = EP(nc.scalar, ACTIVE == "scalar")
        G = EP(nc.gpsimd, ACTIVE == "gpsimd")
        SY = EP(nc.sync, ACTIVE == "sync")
        e2s = {id(T): "mT", id(V): "mV", id(S): "mS", id(G): "mG", id(SY): "mY"}
        grd = {"tensor": T, "vector": V, "scalar": S}.get(ACTIVE)

        def inc(instr, eng):
            s = e2s[id(eng)]
            cnt[s] += 1
            instr.then_inc(sems[s], 1)
            eng.wait_ge(sems[s], cnt[s])
            return cnt[s]

        def dinc(instr, q):
            cnt[q] += 16
            instr.then_inc(sems[q], 16)
            return cnt[q]

        def wt(eng, s, val):
            if val > 0:
                eng.wait_ge(sems[s], val)

        ENGS = [("mT", T), ("mV", V), ("mS", S), ("mG", G), ("mY", SY)]

        def barrier():
            for nm, _e in ENGS:
                cnt[nm] += 1
            me = EP.__getattribute__(
                {"tensor": T, "vector": V, "scalar": S, "gpsimd": G, "sync": SY}[ACTIVE],
                "_r") if False else None
            ap = {"tensor": T, "vector": V, "scalar": S, "gpsimd": G, "sync": SY}[ACTIVE]
            ap.nop().then_inc(sems["m" + {"tensor": "T", "vector": "V", "scalar": "S",
                                          "gpsimd": "G", "sync": "Y"}[ACTIVE]], 1)
            for nm, _e in ENGS:
                if nm != "m" + {"tensor": "T", "vector": "V", "scalar": "S",
                                "gpsimd": "G", "sync": "Y"}[ACTIVE]:
                    ap.wait_ge(sems[nm], cnt[nm])
            for dq in DMA_SEMS:
                if cnt[dq]:
                    ap.wait_ge(sems[dq], cnt[dq])

        class _EarlyStop(Exception):
            pass

        def maybe_stop(tag):
            if STOP_AFTER == tag:
                barrier()
                gd = G.dma_start(out=y_d[0:128, 0:128], in_=identb[:])
                dinc(gd, "g0")
                wt(G, "g0", cnt["g0"])
                wt(SY, "g0", cnt["g0"])
                raise _EarlyStop()

        for p in range(4):
            dinc(SY.dma_start(out=masksb[:, p, :], in_=masks_d[p * 128:(p + 1) * 128, :]), "c0")
        dinc(SY.dma_start(out=onesb[:], in_=ones_d), "c0")
        dinc(SY.dma_start(out=identb[:], in_=ident_d), "c0")
        cid_dd = dinc(SY.dma_start(out=cids[0:1, 0:4], in_=cid_d), "c0")
        wt(V, "c0", cnt["c0"])
        inc(V.memset(acc3[:], 0.0), V)
        inc(V.memset(cpool[:, 0:1], EPS), V)
        inc(V.memset(cpool[:, 1:2], CB), V)
        inc(V.memset(cpool[:, 2:3], EXPB), V)
        inc(V.memset(cpool[:, 3:4], F16B), V)
        inc(V.memset(onesf[:], 1.0), V)
        vcp = inc(V.tensor_copy(out=identf[:], in_=identb[:]), V)
        wt(S, "mV", vcp)

        maybe_stop("CONST")
        pre_dd = [dinc(SY.dma_start(out=xt0[:], in_=xk_d[0:128, :]), "x0"),
                  dinc(SY.dma_start(out=xt1[:], in_=xk_d[128:256, :]), "x1")]
        for h2 in range(2):
            sem = "w0" if h2 == 0 else "w1"
            if h2 == 1:
                wt(SY, "mV", cnt["mV"])
            dd = dinc(SY.dma_start(
                out=wdst[:],
                in_=wdkv_d[:, h2 * 256:(h2 + 1) * 256].rearrange("(f p) c -> p f c", p=128)), sem)
            wt(V, sem, dd)
            inc(V.tensor_copy(out=wdkv[:, :, h2 * 256:(h2 + 1) * 256], in_=wdst[:]), V)
        maybe_stop("WDKV")

        def ln_stats(buf, col, dmy=None):
            dmy = statda if dmy is None else dmy
            a = S.activation(out=dmy[:], in_=buf[:], func=AF.Square,
                             accum_out=sumsq_c[:, col:col + 1])
            inc(a, S)
            a = S.activation(out=dmy[:], in_=buf[:], func=AF.Identity,
                             accum_out=sums_c[:, col:col + 1])
            return inc(a, S)

        def ln_norm(buf, col, st_sm, own_idx=None):
            wt(V, "mS", st_sm)
            inc(V.tensor_scalar(out=lnt0[:, col:col + 1], in0=sums_c[:, col:col + 1],
                                scalar1=1.0 / D, scalar2=None, op0=ALU.mult), V)
            inc(V.scalar_tensor_tensor(out=lnt1[:, col:col + 1], in0=lnt0[:, col:col + 1],
                                       scalar=1.0, in1=lnt0[:, col:col + 1],
                                       op0=ALU.mult, op1=ALU.mult), V)
            vm = inc(V.scalar_tensor_tensor(out=lnt1[:, col:col + 1], in0=sumsq_c[:, col:col + 1],
                                            scalar=1.0 / D, in1=lnt1[:, col:col + 1],
                                            op0=ALU.mult, op1=ALU.subtract), V)
            wt(S, "mV", vm)
            inc(S.activation(out=lnt1[:, col:col + 1], in_=lnt1[:, col:col + 1],
                             func=AF.Ln, bias=EPS), S)
            sm = inc(S.activation(out=lnt1[:, col:col + 1], in_=lnt1[:, col:col + 1],
                                  func=AF.Exp, scale=-0.5), S)
            wt(V, "mS", sm)
            vm = inc(V.scalar_tensor_tensor(out=lnt2[:, col:col + 1], in0=lnt0[:, col:col + 1],
                                            scalar=-1.0, in1=lnt1[:, col:col + 1],
                                            op0=ALU.mult, op1=ALU.mult), V)
            wt(S, "mV", vm)
            sm = inc(S.activation(out=xntb[:, col % 2, :], in_=buf[:], func=AF.Identity,
                                  scale=lnt1[:, col:col + 1], bias=lnt2[:, col:col + 1]), S)
            if own_idx is not None:
                sm = inc(S.activation(out=xnof[:, own_idx % 2, :], in_=buf[:], func=AF.Identity,
                                      scale=lnt1[:, col:col + 1], bias=lnt2[:, col:col + 1]), S)
            return sm

        tb_ev = {4: 0, 5: 0, 6: 0, 7: 0}

        def transpose_tile(src_tile, dst_fn, wait_s=0, wait_v=0):
            if wait_s:
                wt(T, "mS", wait_s)
            if wait_v:
                wt(T, "mV", wait_v)
            for fc in range(FD):
                b = 4 + (fc % 4)
                wt(T, "mV", tb_ev[b])
                mm = T.matmul(out=bank(b)[:, 0:128],
                              lhsT=src_tile[:, fc * 128:(fc + 1) * 128],
                              rhs=identb[:], start=True, stop=True)
                tm = inc(mm, T)
                wt(V, "mT", tm)
                ev = V.tensor_copy(out=dst_fn(fc), in_=bank(b)[:, 0:128])
                tb_ev[b] = inc(ev, V)

        scale_blocks = []
        for mi, wsrc in enumerate((wg_d, wu_d, wd_d)):
            for rb in range(wsrc.shape[0] // 128):
                for sb in range(wsrc.shape[1] // D):
                    scale_blocks.append((wsrc, rb, sb, mi))
        sc_i = {"v": 0}
        sc_done = {}

        def pump_scale(n, bufs=None, on_s=False, dq=None, psem="p0"):
            if bufs is None:
                bufs = (xt0, xt1)
            if dq is None:
                dq = G
            for _ in range(n):
                k = sc_i["v"]
                if k >= len(scale_blocks):
                    return
                sc_i["v"] += 1
                wsrc, rb, sb, mi = scale_blocks[k]
                slot = k % len(bufs)
                sbuf = bufs[slot]
                ps = psem if len(bufs) == 1 else ("p0" if slot == 0 else "p1")
                col = k - 64 * mi
                prev = sc_done.get(id(sbuf))
                if prev:
                    wt(dq, prev[0], prev[1])
                lastp = sc_done.get(ps)
                if lastp and lastp != prev:
                    wt(dq, lastp[0], lastp[1])
                dd = dinc(dq.dma_start(out=sbuf[:],
                                       in_=wsrc[rb * 128:(rb + 1) * 128, sb * D:(sb + 1) * D]),
                          ps)
                if on_s:
                    wt(S, ps, dd)
                    sm = inc(S.activation(out=sbuf[:], in_=sbuf[:], func=AF.Abs,
                                          accum_out=accTw[:, mi, col:col + 1]), S)
                    sc_done[id(sbuf)] = sc_done[ps] = ("mS", sm)
                else:
                    wt(V, ps, dd)
                    vm = inc(V.reduce_sum(out=accTw[:, mi, col:col + 1], in_=sbuf[:], axis=AX,
                                          apply_absolute_value=True), V)
                    sc_done[id(sbuf)] = sc_done[ps] = ("mV", vm)

        xn_sm = [0] * (NKC + NQT)
        tp_done = [0] * (NKC + NQT)
        dkv_done = [0] * 12
        st_sm = [0] * (NKC + NQT)
        dds = [0] * NKC
        dds[0], dds[1] = pre_dd
        wt(S, "x0", dds[0])
        st_sm[0] = ln_stats(xt0, 0)
        for it in range(NKC):
            buf = xt0 if it % 2 == 0 else xt1
            if it + 1 < NKC:
                nbuf = xt1 if it % 2 == 0 else xt0
                nsem = "x1" if it % 2 == 0 else "x0"
                if it + 1 >= 2:
                    wt(SY, "mS", xn_sm[it - 1])
                    dds[it + 1] = dinc(SY.dma_start(out=nbuf[:],
                                                    in_=xk_d[(it + 1) * 128:(it + 2) * 128, :]), nsem)
                wt(S, nsem, dds[it + 1])
                st_sm[it + 1] = ln_stats(nbuf, it + 1)
            if it >= 2:
                wt(S, "mV", tp_done[it - 2])
            xn_sm[it] = ln_norm(buf, it, st_sm[it])
            gi, sl = it // 4, it % 4

            def dstf(fc, gi=gi, sl=sl):
                return xnf[:, gi % 2, fc, sl * 128:(sl + 1) * 128]

            wv = dkv_done[gi - 2] if gi >= 2 and sl == 0 else 0
            transpose_tile(xntb[:, it % 2, :], dstf, wait_s=xn_sm[it], wait_v=wv)
            tp_done[it] = cnt["mV"]
            if sl == 3:
                wt(T, "mV", tp_done[it])
                for cc in range(KVCD):
                    for fc in range(FD):
                        mm = T.matmul(out=bank(cc), lhsT=wdkv[:, fc, cc * 128:(cc + 1) * 128],
                                      rhs=xnf[:, gi % 2, fc, :],
                                      start=(fc == 0), stop=(fc == FD - 1))
                    tm = inc(mm, T)
                    wt(V, "mT", tm)
                    ev = V.tensor_copy(out=ckv[:, cc, gi * 512:(gi + 1) * 512], in_=bank(cc))
                    dkv_done[gi] = inc(ev, V)
            pump_scale(1, (pa0, pa1))
        barrier()

        maybe_stop('P0a')
        xq_vm = [0] * NQT
        bdd = [0] * NQT
        bdd[0] = dinc(SY.dma_start(out=xt0[:], in_=xo_d[0:128, :]), "x0")
        wt(S, "x0", bdd[0])
        st_sm[NKC] = ln_stats(xt0, NKC, dmy=qu_g)
        for it in range(NQT):
            buf = xt0 if it % 2 == 0 else xt1
            if it + 1 < NQT:
                nbuf = xt1 if it % 2 == 0 else xt0
                nsem = "x1" if it % 2 == 0 else "x0"
                if it + 1 >= 2:
                    wt(SY, "mS", xn_sm[NKC + it - 1])
                bdd[it + 1] = dinc(SY.dma_start(out=nbuf[:],
                                                in_=xo_d[(it + 1) * 128:(it + 2) * 128, :]), nsem)
                wt(S, nsem, bdd[it + 1])
                st_sm[NKC + it + 1] = ln_stats(nbuf, NKC + it + 1, dmy=qu_g)
            if it >= 2:
                wt(S, "mV", tp_done[NKC + it - 2])
            if it >= 1:
                wt(S, "mV", xq_vm[it - 1])
            xn_sm[NKC + it] = ln_norm(buf, NKC + it, st_sm[NKC + it], own_idx=it)
            sm = xn_sm[NKC + it]
            wt(V, "mS", sm)
            inc(V.reduce_max(out=xnmax[:, it:it + 1], in_=xnof[:, it % 2, :], axis=AX,
                             apply_absolute_value=True), V)
            inc(V.tensor_scalar(out=scq[:, it:it + 1], in0=xnmax[:, it:it + 1],
                                scalar1=1.0 / 127.0, scalar2=1e-5 / 127.0,
                                op0=ALU.mult, op1=ALU.max), V)
            vm = inc(V.reciprocal(out=scq[:, it:it + 1], in_=scq[:, it:it + 1]), V)
            wt(S, "mV", vm)
            a = S.activation(out=qu_u[:], in_=xnof[:, it % 2, :], func=AF.Identity,
                             scale=scq[:, it:it + 1], bias=F16B)
            sm2 = inc(a, S)
            wt(V, "mS", sm2)
            inc(V.tensor_copy(out=qu_h[:], in_=qu_u[:]), V)
            xq_vm[it] = inc(V.tensor_scalar(out=xqt[:, it, :], in0=qu_h[:], scalar1=-F16B,
                                            scalar2=None, op0=ALU.add), V)

            def dsto(fc, it=it):
                return xnofm[:, fc, it * 128:(it + 1) * 128]

            transpose_tile(xntb[:, (NKC + it) % 2, :], dsto, wait_s=sm)
            tp_done[NKC + it] = cnt["mV"]
        barrier()

        maybe_stop('P0b')
        for cc in range(KVCD):
            for h2 in range(2):
                buf = xt0 if h2 == 0 else xt1
                wt(SY, "mV", max(0, cnt["mV"] - 1))
                dd = dinc(SY.dma_start(out=buf[:], in_=wukv_d[cc * 128:(cc + 1) * 128, h2 * D:(h2 + 1) * D]),
                          "x0" if h2 == 0 else "x1")
                wt(V, "x0" if h2 == 0 else "x1", dd)
                inc(V.tensor_copy(out=wukv[:, cc, h2 * D:(h2 + 1) * D], in_=buf[:]), V)
        xq_dg = [0] * NQT
        for it in range(NQT):
            def dstq(fc, it=it):
                return xnf[:, it % 2, fc, 0:128]

            wv = xq_dg[it - 2] if it >= 2 else 0
            if wv:
                wt(V, "g0" if it % 2 == 0 else "g1", wv)
            transpose_tile(xqt[:, it, :], dstq, wait_v=xq_vm[it])
            vm = cnt["mV"]
            wt(G, "mV", vm)
            gd = G.dma_start(out=xqs_d[:, :, it * 128:(it + 1) * 128].rearrange("f p c -> p f c"),
                             in_=xnf[:, it % 2, :, 0:128])
            xq_dg[it] = dinc(gd, "g0" if it % 2 == 0 else "g1")
            pump_scale(1, (pc0,), on_s=True)
        barrier()

        maybe_stop('P0c')
        wcast_done = {0: 0, 1: 0}

        def stream_blocks(src, nchunks, col0, half, pre=None):
            if pre is None:
                if wcast_done[half]:
                    wt(SY, "mV", wcast_done[half])
                dd = dinc(SY.dma_start(
                    out=wst_f[:, half, 0:nchunks * 128].rearrange("p (f c) -> p f c", c=128),
                    in_=src[0:nchunks * 128, col0:col0 + 128].rearrange("(f p) c -> p f c", p=128)),
                    "w0" if half == 0 else "w1")
            else:
                dd = pre
            wt(V, "w0" if half == 0 else "w1", dd)
            wcast_done[half] = inc(
                V.tensor_copy(out=wst_b[:, half, 0:nchunks, :].rearrange("p f c -> p (f c)"),
                              in_=wst_f[:, half, 0:nchunks * 128]), V)
            return wcast_done[half]

        ev_bank = {}
        for qc in range(QCD):
            wv = stream_blocks(wdq_d, FD, qc * 128, qc % 2)
            wt(T, "mV", wv)
            for half in range(2):
                b = (qc * 2 + half) % 4
                wt(T, "mV", ev_bank.get(b, 0))
                for fc in range(FD):
                    mm = T.matmul(out=bank(b), lhsT=wst_b[:, qc % 2, fc, :],
                                  rhs=xnofm[:, fc, half * 512:(half + 1) * 512],
                                  start=(fc == 0), stop=(fc == FD - 1))
                tm = inc(mm, T)
                wt(V, "mT", tm)
                ev = V.tensor_copy(out=qcfm[:, qc, half * 512:(half + 1) * 512], in_=bank(b))
                ev_bank[b] = inc(ev, V)
            pump_scale(1, (pd0,), on_s=True)
            if qc == QCD - 2:
                wt(SY, "mV", wcast_done[0])
                uq0_dd = dinc(SY.dma_start(
                    out=wst_f[:, 0, 0:QCD * 128].rearrange("p (f c) -> p f c", c=128),
                    in_=wuq_d[0:QCD * 128, 0:128].rearrange("(f p) c -> p f c", p=128)), "w0")
        wt(G, "mT", cnt["mT"])
        for oc in range(FD):
            wv = stream_blocks(wuq_d, QCD, oc * 128, oc % 2,
                               pre=uq0_dd if oc == 0 else None)
            wt(T, "mV", wv)
            for half in range(2):
                b = (oc * 2 + half) % 4
                wt(T, "mV", ev_bank.get(b, 0))
                for qc in range(QCD):
                    mm = T.matmul(out=bank(b), lhsT=wst_b[:, oc % 2, qc, :],
                                  rhs=qcfm[:, qc, half * 512:(half + 1) * 512],
                                  start=(qc == 0), stop=(qc == QCD - 1))
                tm = inc(mm, T)
                wt(V, "mT", tm)
                ev = V.tensor_copy(out=qfm[:, oc, half * 512:(half + 1) * 512], in_=bank(b))
                ev_bank[b] = inc(ev, V)
            pump_scale(2, (pe0, pe1), on_s=True)
        p0e_T = cnt["mT"]
        p0e_V = cnt["mV"]
        p0e_S = cnt["mS"]

        maybe_stop('P0de')
        cid_reg = grd.alloc_register("cidreg") if grd else None
        if grd:
            wt(grd, "c0", cid_dd)
            grd.reg_load(cid_reg, cids[0:1, 0:1])

        def emit_attn_sweep(cid, h0, kv_ready, fin_in=0, gate_s=0):
            nA = 4 * (cid + 1)
            half_first = {0: 0, 1: nA}
            half_last = {0: nA - 1, 1: NKC - 1}
            t0, v0, s0 = cnt["mT"], cnt["mV"], cnt["mS"]
            jlo = 0
            wt(T, "mV", kv_ready)
            wt(T, "mV", fin_in)
            wt(V, "mS", gate_s)
            exp_sm = {}
            post_tm = {}

            def boundary(half):
                wt(V, "mT", cnt["mT"])
                inc(V.reciprocal(out=recip_s[0:64, 0:512], in_=psb6[0:64, :]), V)
                vm1 = inc(V.reciprocal(out=recip_s[0:64, 512:1024], in_=psb7[0:64, :]), V)
                fin = 0
                for hh in range(HPG):
                    wt(T, "mV", vm1 if hh == 0 else fin)
                    mm = T.matmul(out=bank(7), lhsT=onesf[0:1, :],
                                  rhs=recip_s[0:1, 512 * hh:512 * hh + 512],
                                  start=True, stop=True)
                    tmr = inc(mm, T)
                    wt(V, "mT", tmr)
                    inc(V.tensor_copy(out=rep_s[:, hh, :], in_=bank(7)), V)
                    ev = V.tensor_tensor(out=otall[:, h0 + hh, half * 512:(half + 1) * 512],
                                         in0=bank(hh), in1=rep_s[:, hh, :], op=ALU.mult)
                    fin = inc(ev, V)
                return fin

            fin_vm = 0
            nb = 0
            nmask = 0

            def emit_scores(j):
                half = 0 if j < nA else 1
                qc0 = half * 512
                sp = 2 + 2 * ((j - jlo) % 2)
                if (j - jlo) >= 2:
                    wt(T, "mS", exp_sm[j - 2])
                for hh in range(HPG):
                    mm = T.matmul(out=bank(sp + hh), lhsT=kg[:, hh, (j - jlo) * 128:(j - jlo + 1) * 128],
                                  rhs=qfm[:, h0 + hh, qc0:qc0 + 512], start=True, stop=True)
                tm = inc(mm, T)
                wt(S, "mT", tm)
                if (j - jlo) >= 2:
                    wt(S, "mT", post_tm[j - 2])
                pb = (j - jlo) % 2
                a = S.activation(out=ptile[:, pb, :, :].rearrange("p h c -> p (h c)"),
                                 in_=bank2(sp), func=AF.Exp, scale=SCALE, bias=EXPB)
                exp_sm[j] = inc(a, S)

            emit_scores(0)
            for j in range(NKC):
                half = 0 if j < nA else 1
                pb = (j - jlo) % 2
                if j + 1 < NKC and j + 1 != nA:
                    emit_scores(j + 1)
                if half == 0:
                    mp = j - (nA - 4)
                else:
                    mp = j - 32
                use_mask = 0 <= mp < 4
                if use_mask:
                    nmask += 1
                    wt(V, "mS", exp_sm[j])
                    inc(V.tensor_tensor(out=ptile[:, pb, 0, :], in0=ptile[:, pb, 0, :],
                                        in1=masksb[:, mp, :], op=ALU.mult), V)
                    pvm = inc(V.tensor_tensor(out=ptile[:, pb, 1, :], in0=ptile[:, pb, 1, :],
                                              in1=masksb[:, mp, :], op=ALU.mult), V)
                    wt(T, "mV", pvm)
                else:
                    wt(T, "mS", exp_sm[j])
                if fin_vm and j == nA:
                    wt(T, "mV", fin_vm)
                st = (j == half_first[half])
                sp_ = (j == half_last[half])
                for hh in range(HPG):
                    T.matmul(out=(psb6 if hh == 0 else psb7)[0:64, :],
                             lhsT=onesb[:, 0:64], rhs=ptile[:, pb, hh, :],
                             start=st, stop=sp_)
                for hh in range(HPG):
                    mm = T.matmul(out=bank(hh), lhsT=vg[:, j - jlo, hh * 128:(hh + 1) * 128],
                                  rhs=ptile[:, pb, hh, :], start=st, stop=sp_)
                post_tm[j] = inc(mm, T)
                if sp_:
                    fin_vm = boundary(half)
                    nb += 1
                    if j + 1 < NKC and j + 1 == nA:
                        emit_scores(j + 1)
            maxT = 2 * NKC + 4
            maxV = 16 + 12
            maxS = NKC
            dT = (t0 + maxT) - cnt["mT"]
            dV = (v0 + maxV) - cnt["mV"]
            dS = (s0 + maxS) - cnt["mS"]
            assert dT >= 0 and dV >= 0 and dS >= 0, (cid, dT, dV, dS)
            for _ in range(dT):
                inc(T.nop(), T)
            for _ in range(dV):
                inc(V.nop(), V)
            for _ in range(dS):
                inc(S.nop(), S)
            return fin_vm

        fin_prev = {c: 0 for c in range(NCORES)}
        ms_sweeps = 0
        for g in range(NGRP):
            h0 = g * HPG
            if ms_sweeps:
                wt(T, "mS", ms_sweeps)
            if g == 0:
                kv_ev = {2: ev_bank.get(2, 0), 3: ev_bank.get(3, 0),
                         4: tb_ev[4], 5: tb_ev[5]}
                wt(G, "mS", p0e_S)
                wt(G, "mV", p0e_V)
                wt(T, "mV", p0e_V)
            else:
                kv_ev = {}
            evi = 0
            for kt in range(9):
                k0 = kt * 512
                for hh in range(HPG):
                    b = 2 + (2 * kt + hh) % 4
                    wt(T, "mV", kv_ev.get(b, 0))
                    for cc in range(KVCD):
                        mm = T.matmul(out=bank(b),
                                      lhsT=wukv[:, cc, (h0 + hh) * 128:(h0 + hh + 1) * 128],
                                      rhs=ckv[:, cc, k0:k0 + 512],
                                      start=(cc == 0), stop=(cc == KVCD - 1))
                    tm = inc(mm, T)
                    wt(V, "mT", tm)
                    ev = V.tensor_copy(out=kg[:, hh, k0:k0 + 512], in_=bank(b))
                    kv_ev[b] = inc(ev, V)
                    evi += 1
                    if evi % 4 == 0:
                        pump_scale(1)
            for jj in range(NKC):
                b = 2 + (jj % 4)
                wt(T, "mV", kv_ev.get(b, 0))
                for cc in range(KVCD):
                    mm = T.matmul(out=bank(b)[:, 0:HPG * 128],
                                  lhsT=ckv[:, cc, jj * 128:(jj + 1) * 128],
                                  rhs=wukv[:, cc, 2048 + h0 * 128:2048 + (h0 + HPG) * 128],
                                  start=(cc == 0), stop=(cc == KVCD - 1))
                tm = inc(mm, T)
                wt(V, "mT", tm)
                ev = V.tensor_copy(out=vg[:, jj, :], in_=bank(b)[:, 0:HPG * 128])
                kv_ev[b] = inc(ev, V)
                evi += 1
                if evi % 4 == 0 and evi <= 52:
                    pump_scale(1)
            kv_ready = cnt["mV"]

            base_cnt = dict(cnt)
            finals = []
            fin_out = {}
            for cid in range(NCORES):
                for kk in cnt:
                    cnt[kk] = base_cnt[kk]
                with (grd.If_eq(cid_reg, cid) if grd else _NG):
                    fin_out[cid] = emit_attn_sweep(cid, h0, kv_ready, fin_prev[cid],
                                                   gate_s=p0e_S if g == 0 else 0)
                finals.append(dict(cnt))
            for f in finals:
                assert f == finals[0], (finals[0], f)
            for kk in cnt:
                cnt[kk] = finals[0][kk]
            fin_prev = fin_out
            ms_sweeps = cnt["mS"]
            maybe_stop("G%d" % g)
        barrier()

        for b_ in (xt0, xt1):
            pv = sc_done.get(id(b_))
            if pv:
                wt(SY, pv[0], pv[1])
        aos_dg = [0] * FD
        wcast_done = {0: 0, 1: 0}
        for oc in range(FD):
            half = oc % 2
            if wcast_done[half]:
                wt(SY, "mV", wcast_done[half])
            dd = dinc(SY.dma_start(
                out=wof[:, half, :].rearrange("p (f c) -> p f c", c=128),
                in_=wout_d[:, oc * 128:(oc + 1) * 128].rearrange("(f p) c -> p f c", p=128)),
                "w0" if half == 0 else "w1")
            wt(V, "w0" if half == 0 else "w1", dd)
            wv = wcast_done[half] = inc(
                V.tensor_copy(out=wob[:, half, :, :].rearrange("p f c -> p (f c)"),
                              in_=wof[:, half, :]), V)
            wt(T, "mV", wv)
            for half in range(2):
                b = (oc * 2 + half) % 4
                wt(T, "mV", ev_bank.get(b, 0))
                for ic in range(FD):
                    mm = T.matmul(out=bank(b), lhsT=wob[:, oc % 2, ic, :],
                                  rhs=otall[:, ic, half * 512:(half + 1) * 512],
                                  start=(ic == 0), stop=(ic == FD - 1))
                tm = inc(mm, T)
                wt(V, "mT", tm)
                if oc >= 2 and half == 0:
                    wt(V, "g0" if oc % 2 == 0 else "g1", aos_dg[oc - 2])
                ev = V.tensor_copy(out=aostg[:, oc % 2, half * 512:(half + 1) * 512], in_=bank(b))
                ev_bank[b] = inc(ev, V)
            wt(G, "mV", cnt["mV"])
            gd = G.dma_start(out=aos_d[oc], in_=aostg[:, oc % 2, :])
            aos_dg[oc] = dinc(gd, "g0" if oc % 2 == 0 else "g1")
        wt(SY, "mT", cnt["mT"])
        wt(SY, "mV", wcast_done[0])
        wt(SY, "mV", wcast_done[1])
        gsrc = wg_d.rearrange("(fc p) (hc c) -> fc p hc c", p=128, c=512)
        usrc = wu_d.rearrange("(fc p) (hc c) -> fc p hc c", p=128, c=512)
        p2a_xq = dinc(SY.dma_start(out=xqfm[:], in_=xqs_d.rearrange("f p c -> p f c")), "c0")
        p2a_g0 = dinc(SY.dma_start(out=gsq[:, 0, :, :],
                                   in_=gsrc[0:4, :, 0, :].rearrange("f p c -> p f c")), "w0")
        p2a_u0 = dinc(SY.dma_start(out=usq[:, 0, :, :],
                                   in_=usrc[0:4, :, 0, :].rearrange("f p c -> p f c")), "x0")
        wt(G, "mT", cnt["mT"])
        pump_scale(10 ** 9, bufs=(pt0, pt1), on_s=True)

        maybe_stop('P1out')
        wt(V, "mS", cnt["mS"])
        for mi in range(3):
            inc(V.reduce_sum(out=acc3[:, mi:mi + 1], in_=accTw[:, mi, :], axis=AX), V)
        vm = inc(V.tensor_copy(out=auxs[:, 0:4], in_=acc3[:, 0:4]), V)
        wt(T, "mV", vm)
        tm = inc(T.matmul(out=bank(0)[0:1, 0:4], lhsT=onesf[:, 0:1], rhs=auxs[:, 0:4],
                          start=True, stop=True), T)
        wt(V, "mT", tm)
        vm = inc(V.tensor_copy(out=auxs[0:1, 4:8], in_=bank(0)[0:1, 0:4]), V)
        wt(T, "mV", vm)
        tm = inc(T.matmul(out=bank(1)[:, 0:4], lhsT=onesf[0:1, :], rhs=auxs[0:1, 4:8],
                          start=True, stop=True), T)
        wt(V, "mT", tm)
        inc(V.tensor_copy(out=auxs[:, 8:12], in_=bank(1)[:, 0:4]), V)
        for (col, src, mul) in ((0, 0, 0.5 / NW), (1, 0, -0.5 / NW), (2, 1, 0.5 / NW),
                                (3, 1, -0.5 / NW), (4, 2, 0.5 / NW), (5, 2, -0.5 / NW),
                                (6, 0, 1.0 / (NW * 127.0))):
            inc(V.tensor_scalar(out=thr[:, col:col + 1], in0=auxs[:, 8 + src:9 + src],
                                scalar1=mul, scalar2=None, op0=ALU.mult), V)
        inc(V.scalar_tensor_tensor(out=thr[:, 7:8], in0=auxs[:, 10:11],
                                   scalar=1.0 / (NW * NW * 127.0 * 127.0),
                                   in1=auxs[:, 9:10], op0=ALU.mult, op1=ALU.mult), V)
        inc(V.tensor_scalar(out=sgs[:], in0=xnmax[:], scalar1=thr[:, 6:7],
                            scalar2=None, op0=ALU.mult), V)

        maybe_stop('SCL')
        wt(T, "c0", p2a_xq)
        wt(S, "g0", aos_dg[FD - 2])
        wt(S, "g1", aos_dg[FD - 1])

        hs_dg = [0] * FD
        tq_used = {0: 0, 1: 0}
        sq_done = {}
        silu_sm = [0] * NQT
        ev_tb = [0] * NQT
        pend = None
        for hc in range(FD):
            setq = hc % 2
            tg = tgq if setq == 0 else tgq2
            tu = tuq if setq == 0 else tuq2
            qg_q = [0] * 4
            qu_q = [0] * 4
            for q in range(4):
                gi = (hc * 4 + q) % 2
                gsem = "w0" if gi == 0 else "w1"
                usem = "x0" if gi == 0 else "x1"
                if hc == 0 and q == 0:
                    ddg, ddu = p2a_g0, p2a_u0
                else:
                    if sq_done.get(("g", gi)):
                        wt(SY, "mV", sq_done[("g", gi)])
                    ddg = dinc(SY.dma_start(out=gsq[:, gi, :, :],
                                            in_=gsrc[4 * q:4 * q + 4, :, hc, :].rearrange("f p c -> p f c")), gsem)
                    if sq_done.get(("u", gi)):
                        wt(SY, "mV", sq_done[("u", gi)])
                    ddu = dinc(SY.dma_start(out=usq[:, gi, :, :],
                                            in_=usrc[4 * q:4 * q + 4, :, hc, :].rearrange("f p c -> p f c")), usem)
                wt(V, gsem, ddg)
                if q == 0:
                    wt(V, "mT", tq_used[setq])
                fl = gsq[:, gi, :, :].rearrange("p f c -> p (f c)")
                fa = cmp_a[:, 0:4, :].rearrange("p f c -> p (f c)")
                inc(V.tensor_scalar(out=fa, in0=fl, scalar1=thr[:, 1:2],
                                    scalar2=None, op0=ALU.is_lt), V)
                fqg = tg[:, 4 * q:4 * q + 4, :].rearrange("p f c -> p (f c)")
                qg_q[q] = sq_done[("g", gi)] = inc(V.scalar_tensor_tensor(
                    out=fqg, in0=fl, scalar=thr[:, 0:1], in1=fa,
                    op0=ALU.is_gt, op1=ALU.subtract), V)
                wt(V, usem, ddu)
                flu = usq[:, gi, :, :].rearrange("p f c -> p (f c)")
                inc(V.tensor_scalar(out=fa, in0=flu, scalar1=thr[:, 3:4],
                                    scalar2=None, op0=ALU.is_lt), V)
                fqu = tu[:, 4 * q:4 * q + 4, :].rearrange("p f c -> p (f c)")
                qu_q[q] = sq_done[("u", gi)] = inc(V.scalar_tensor_tensor(
                    out=fqu, in0=flu, scalar=thr[:, 2:3], in1=fa,
                    op0=ALU.is_gt, op1=ALU.subtract), V)
            if pend is not None:
                ehc, etmu = pend
                for tb in range(NQT):
                    wt(V, "mT", etmu[tb])
                    if ehc >= 1 and tb == 0:
                        wt(V, "g2", hs_dg[ehc - 1])
                    inc(V.tensor_tensor(out=hstg[:, tb, :], in0=bank(tb), in1=hg[:, tb, :],
                                        op=ALU.mult), V)
                    ev_tb[tb] = cnt["mV"]
                    inc(V.reduce_max(out=hmaxa[:, ehc, tb:tb + 1], in_=hstg[:, tb, :], axis=AX,
                                     apply_absolute_value=True), V)
                wt(G, "mV", cnt["mV"])
                gd = G.dma_start(out=hs_d.rearrange("(tb p) (hc c) -> tb p hc c", p=128, c=512)[:, :, ehc, :].rearrange("tb p c -> p tb c"),
                                 in_=hstg[:])
                hs_dg[ehc] = dinc(gd, "g2")
            tmg = [0] * NQT
            for q in range(4):
                wt(T, "mV", qg_q[q])
                for f4 in range(4):
                    fc = 4 * q + f4
                    for tb in range(NQT):
                        if fc == 0 and hc >= 1:
                            wt(T, "mV", ev_tb[tb])
                        mm = T.matmul(out=bank(tb), lhsT=xqfm[:, fc, tb * 128:(tb + 1) * 128],
                                      rhs=tg[:, fc, :], start=(fc == 0), stop=(fc == FD - 1))
                        if fc == FD - 1:
                            tmg[tb] = inc(mm, T)
            for tb in range(NQT):
                wt(S, "mT", tmg[tb])
                silu_sm[tb] = inc(S.activation(out=hg[:, tb, :], in_=bank(tb), func=AF.Silu,
                                               scale=sgs[:, tb:tb + 1]), S)
            tmu = [0] * NQT
            for q in range(4):
                wt(T, "mV", qu_q[q])
                for f4 in range(4):
                    fc = 4 * q + f4
                    for tb in range(NQT):
                        if fc == 0:
                            wt(T, "mS", silu_sm[tb])
                        mm = T.matmul(out=bank(tb), lhsT=xqfm[:, fc, tb * 128:(tb + 1) * 128],
                                      rhs=tu[:, fc, :], start=(fc == 0), stop=(fc == FD - 1))
                        if fc == FD - 1:
                            tmu[tb] = inc(mm, T)
            tq_used[setq] = cnt["mT"]
            pend = (hc, list(tmu))
        ehc, etmu = pend
        for tb in range(NQT):
            wt(V, "mT", etmu[tb])
            if tb == 0:
                wt(V, "g2", hs_dg[ehc - 1])
            inc(V.tensor_tensor(out=hstg[:, tb, :], in0=bank(tb), in1=hg[:, tb, :],
                                op=ALU.mult), V)
            inc(V.reduce_max(out=hmaxa[:, ehc, tb:tb + 1], in_=hstg[:, tb, :], axis=AX,
                             apply_absolute_value=True), V)
        wt(G, "mV", cnt["mV"])
        gd = G.dma_start(out=hs_d.rearrange("(tb p) (hc c) -> tb p hc c", p=128, c=512)[:, :, ehc, :].rearrange("tb p c -> p tb c"),
                         in_=hstg[:])
        hs_dg[ehc] = dinc(gd, "g2")
        for b_ in (4, 5, 6, 7):
            tb_ev[b_] = cnt["mV"]

        maybe_stop('P2a')
        s_last = [0] * NQT
        act_q = [0, 0, 0, 0]
        dqd = [0, 0, 0, 0]
        QS = ("w0", "w1", "x0", "x1")
        for vv in sorted(sq_done.values()):
            wt(SY, "mV", vv)
        for tb in range(NQT):
            for qt in range(4):
                if tb == 0:
                    wt(SY, "g2", hs_dg[4 * qt + 3])
                if tb >= 1:
                    wt(SY, "mS", act_q[qt])
                dqd[qt] = dinc(SY.dma_start(out=hbuf[:, qt * D:(qt + 1) * D],
                                            in_=hs_d[tb * 128:(tb + 1) * 128, qt * D:(qt + 1) * D]),
                               QS[qt])
            if tb >= 1:
                wt(V, "mS", s_last[tb - 1])
            inc(V.reduce_max(out=hmax[:, tb:tb + 1],
                             in_=hmaxa[:, :, tb:tb + 1].rearrange("p f c -> p (f c)"),
                             axis=AX), V)
            inc(V.tensor_scalar(out=auxs[:, 8:9], in0=hmax[:, tb:tb + 1],
                                scalar1=1.0 / 127.0, scalar2=None, op0=ALU.mult), V)
            vm = inc(V.reciprocal(out=auxs[:, 9:10], in_=auxs[:, 8:9]), V)
            wt(S, "mV", vm)
            for qt in range(4):
                sl = slice(qt * D, (qt + 1) * D)
                wt(S, QS[qt], dqd[qt])
                a = S.activation(out=q2_u[:], in_=hbuf[:, sl], func=AF.Identity,
                                 scale=auxs[:, 9:10], bias=F16B)
                sm = inc(a, S)
                act_q[qt] = sm
                s_last[tb] = sm
                wt(V, "mS", sm)
                inc(V.tensor_copy(out=q2_h[:], in_=q2_u[:]), V)
                vm = inc(V.tensor_scalar(out=hqtm[:], in0=q2_h[:], scalar1=-F16B,
                                         scalar2=None, op0=ALU.add), V)
                wt(T, "mV", vm)
                for i4 in range(4):
                    b = 4 + i4
                    wt(T, "mV", tb_ev[b])
                    for sl4 in range(4):
                        i2 = i4 * 4 + sl4
                        mm = T.matmul(out=bank(b)[:, sl4 * 128:(sl4 + 1) * 128],
                                      lhsT=hqtm[:, i2 * 128:(i2 + 1) * 128],
                                      rhs=identb[:], start=True, stop=True)
                    tm = inc(mm, T)
                    wt(V, "mT", tm)
                    ev = V.tensor_copy(
                        out=hqfm[:, qt * FD + i4 * 4:qt * FD + i4 * 4 + 4, tb * 128:(tb + 1) * 128],
                        in_=bank(b)[:].rearrange("p (f c) -> p f c", c=128))
                    tb_ev[b] = inc(ev, V)
                wt(S, "mV", vm)
        inc(V.tensor_tensor(out=dscl[:], in0=hmax[:], in1=xnmax[:], op=ALU.mult), V)
        inc(V.tensor_scalar(out=dscl[:], in0=dscl[:], scalar1=thr[:, 7:8],
                            scalar2=None, op0=ALU.mult), V)
        p2b_V = cnt["mV"]
        p2b_S = cnt["mS"]

        maybe_stop('P2b')
        dsrc = wd_d.rearrange("(hc p) (oc c) -> hc p oc c", p=128, c=128)
        wt(SY, "mS", p2b_S)
        wt(V, "mS", p2b_S)
        ev_bank[4] = p2b_V
        tmq_prev = [0, 0, 0, 0]
        e_done = {0: 0, 1: 0}
        pend_c = None
        for oc in range(FD):
            qd_q = [0] * 4
            for ei in range(8):
                slab = dslab[:, 0:8, :] if ei % 2 == 0 else dslab[:, 8:16, :]
                ssem = "w0" if ei % 2 == 0 else "w1"
                if e_done[ei % 2]:
                    wt(SY, "mV", e_done[ei % 2])
                dd = dinc(SY.dma_start(out=slab,
                                       in_=dsrc[ei * 8:(ei + 1) * 8, :, oc, :].rearrange("f p c -> p f c")), ssem)
                fl = slab.rearrange("p f c -> p (f c)")
                fa = dcma[:, 0:8, :].rearrange("p f c -> p (f c)")
                fq = tdq[:, ei * 8:(ei + 1) * 8, :].rearrange("p f c -> p (f c)")
                wt(V, ssem, dd)
                if ei % 2 == 0:
                    wt(V, "mT", tmq_prev[ei // 2])
                inc(V.tensor_scalar(out=fa, in0=fl, scalar1=thr[:, 5:6],
                                    scalar2=None, op0=ALU.is_lt), V)
                e_done[ei % 2] = inc(V.scalar_tensor_tensor(
                    out=fq, in0=fl, scalar=thr[:, 4:5], in1=fa,
                    op0=ALU.is_gt, op1=ALU.subtract), V)
                if ei % 2 == 1:
                    qd_q[ei // 2] = e_done[1]
            if pend_c is not None:
                eoc, ebp, etm = pend_c
                wt(V, "mT", etm)
                ev = V.tensor_copy(out=ffnb[:, eoc, :], in_=bank2(ebp))
                ev_bank[ebp] = inc(ev, V)
            bp = 2 if oc % 2 == 0 else 4
            for qt in range(4):
                wt(T, "mV", qd_q[qt])
                if qt == 0:
                    wt(T, "mV", ev_bank.get(bp, 0))
                for nh in range(2):
                    for hc16 in range(FD):
                        hcc = qt * FD + hc16
                        mm = T.matmul(out=bank(bp + nh), lhsT=tdq[:, hcc, :],
                                      rhs=hqfm[:, hcc, nh * 512:(nh + 1) * 512],
                                      start=(hcc == 0), stop=(hcc == 63))
                tmq_prev[qt] = inc(mm, T)
            pend_c = (oc, bp, tmq_prev[3])
        eoc, ebp, etm = pend_c
        wt(V, "mT", etm)
        ev = V.tensor_copy(out=ffnb[:, eoc, :], in_=bank2(ebp))
        ev_bank[ebp] = inc(ev, V)

        maybe_stop('P2c')
        wt(G, "mT", cnt["mT"])
        wt(G, "g0", aos_dg[FD - 2])
        wt(G, "g1", aos_dg[FD - 1])
        ao_dd = dinc(G.dma_start(out=aofm[:], in_=aos_d.rearrange("f p c -> p f c")), "g2")
        wt(SY, "mT", cnt["mT"])
        wt(T, "g2", ao_dd)
        wt(T, "mV", cnt["mV"])
        y_ev = [0] * NQT
        y_dg = [0] * NQT
        s_cp = [0] * NQT
        v_stt = [0] * NQT
        for tb in range(NQT):
            buf = xtb[:, tb % 2, :]
            if tb >= 2:
                wt(SY, "mV", y_ev[tb - 2])
            dd = dinc(SY.dma_start(out=buf, in_=xo_d[tb * 128:(tb + 1) * 128, :]),
                      "x0" if tb % 2 == 0 else "x1")
            if tb >= 1:
                wt(T, "mS", s_cp[tb - 1])
            for oc in range(FD):
                bt = (psb0, psb1, psb6, psb7)[oc // 4]
                tm_ = T.matmul(out=bt[:, (oc % 4) * 128:(oc % 4) * 128 + 128],
                               lhsT=aofm[:, oc, tb * 128:(tb + 1) * 128],
                               rhs=identb[:], start=True, stop=True)
            tm = inc(tm_, T)
            if tb >= 1:
                wt(T, "mV", v_stt[tb - 1])
            for oc in range(FD):
                pt = psp23 if oc < 8 else psp45
                tm2_ = T.matmul(out=pt[:, (oc % 8) * 128:(oc % 8) * 128 + 128],
                                lhsT=ffnb[:, oc, tb * 128:(tb + 1) * 128],
                                rhs=identb[:], start=True, stop=True)
            tm2 = inc(tm2_, T)
            ys = ystg[:, tb % 2, :]
            wt(S, "mT", tm)
            if tb >= 2:
                wt(S, "g0" if tb % 2 == 0 else "g1", y_dg[tb - 2])
            inc(S.activation(out=ys[:, 0:512], in_=psb0[:], func=AF.Copy), S)
            inc(S.activation(out=ys[:, 512:1024], in_=psb1[:], func=AF.Copy), S)
            inc(S.activation(out=ys[:, 1024:1536], in_=psb6[:], func=AF.Copy), S)
            sm4 = inc(S.activation(out=ys[:, 1536:2048], in_=psb7[:], func=AF.Copy), S)
            s_cp[tb] = sm4
            wt(V, "mS", sm4)
            wt(V, "mT", tm2)
            wt(V, "x0" if tb % 2 == 0 else "x1", dd)
            inc(V.scalar_tensor_tensor(out=ys[:, 0:1024], in0=psp23[:],
                                       scalar=dscl[:, tb:tb + 1], in1=ys[:, 0:1024],
                                       op0=ALU.mult, op1=ALU.add), V)
            v_stt[tb] = inc(V.scalar_tensor_tensor(out=ys[:, 1024:2048], in0=psp45[:],
                                                   scalar=dscl[:, tb:tb + 1], in1=ys[:, 1024:2048],
                                                   op0=ALU.mult, op1=ALU.add), V)
            ev = V.tensor_tensor(out=ys, in0=ys, in1=buf, op=ALU.add)
            y_ev[tb] = inc(ev, V)
            wt(G, "mV", y_ev[tb])
            gd = G.dma_start(out=y_d[tb * 128:(tb + 1) * 128, :], in_=ystg[:, tb % 2, :])
            y_dg[tb] = dinc(gd, "g0" if tb % 2 == 0 else "g1")
        wt(G, "g0", cnt["g0"])
        wt(G, "g1", cnt["g1"])
        wt(SY, "g0", cnt["g0"])
        wt(SY, "g1", cnt["g1"])
        return cnt

    with nc.Block() as block:
        counts = {}

        def _sect(name):
            cnt = {k: 0 for k in ("c0", "x0", "x1", "w0", "w1", "g0", "g1", "g2", "p0", "p1",
                                  "mT", "mV", "mS", "mG", "mY")}
            try:
                counts[name] = emit(nc, name, cnt)
            except Exception as ex:
                if type(ex).__name__ == "_EarlyStop":
                    counts[name] = cnt
                else:
                    raise

        @block.sync
        def _(e):
            _sect("sync")

        @block.vector
        def _(e):
            _sect("vector")

        @block.scalar
        def _(e):
            _sect("scalar")

        @block.tensor
        def _(e):
            _sect("tensor")

        @block.gpsimd
        def _(e):
            _sect("gpsimd")

    base = counts["sync"]
    for k2, v2 in counts.items():
        assert v2 == base, (k2, v2, base)
    for cm in sem_cms:
        cm.__exit__(None, None, None)
    return nc


_BF = ml_dtypes.bfloat16


def _consts():
    masks = np.zeros((512, 512), dtype=_BF)
    for p in range(4):
        kk = np.arange(128)[:, None]
        qq = np.arange(512)[None, :]
        masks[p * 128:(p + 1) * 128, :] = (qq >= kk + p * 128).astype(_BF)
    ones = np.ones((128, 128), dtype=_BF)
    ident = np.eye(128, dtype=np.float32).astype(_BF)
    return masks, ones, ident


def kernel(L0_attn_x, L0_attn_w_dq, L0_attn_w_uq, L0_attn_w_dkv, L0_attn_w_ukv,
           L0_attn_w_out, L0_ffn_w_gate, L0_ffn_w_up, L0_ffn_w_down):
    x = np.asarray(L0_attn_x, dtype=np.float32)
    wdqT = np.ascontiguousarray(np.asarray(L0_attn_w_dq, np.float32).T)
    wuqT = np.ascontiguousarray(np.asarray(L0_attn_w_uq, np.float32).T)
    wdkvT = np.ascontiguousarray(np.asarray(L0_attn_w_dkv, np.float32).T)
    wukvT = np.ascontiguousarray(np.asarray(L0_attn_w_ukv, np.float32).T)
    woutT = np.ascontiguousarray(np.asarray(L0_attn_w_out, np.float32).T)
    wgT = np.ascontiguousarray(np.asarray(L0_ffn_w_gate, np.float32).T)
    wuT = np.ascontiguousarray(np.asarray(L0_ffn_w_up, np.float32).T)
    wdT = np.ascontiguousarray(np.asarray(L0_ffn_w_down, np.float32).T)
    masks, ones, ident = _consts()

    in_maps = []
    for c in range(NCORES):
        KA = 512 * (c + 1)
        KBn = NK - KA
        xk = np.ascontiguousarray(np.concatenate([x[0, :KA], x[1, :KBn]], axis=0))
        xo = np.ascontiguousarray(np.concatenate([x[0, KA - 512:KA], x[1, KBn - 512:KBn]], axis=0))
        in_maps.append(dict(xk=xk, xo=xo, wdqT=wdqT, wuqT=wuqT, wdkvT=wdkvT,
                            wukvT=wukvT, woutT=woutT, wgT=wgT, wuT=wuT, wdT=wdT,
                            masks=masks, onesc=ones, identc=ident,
                            cidc=np.full((1, 4), c, np.int32)))

    nc = build_nc()
    res = run_bass_kernel_spmd(nc, in_maps, core_ids=list(range(NCORES)))

    out = np.empty((2, 4096, D), dtype=np.float32)
    for c in range(NCORES):
        y = np.asarray(res.results[c]["y"], dtype=np.float32)
        KA = 512 * (c + 1)
        KBn = NK - KA
        out[0, KA - 512:KA] = y[0:512]
        out[1, KBn - 512:KBn] = y[512:1024]
    return out


# version 18
# speedup vs baseline: 1.2192x; 1.2192x over previous
import math
import numpy as np
import ml_dtypes
import concourse.bass as bass
import concourse.mybir as mybir
from concourse.bass_utils import run_bass_kernel_spmd

F32 = mybir.dt.float32
BF16 = mybir.dt.bfloat16
F16 = mybir.dt.float16
I32 = mybir.dt.int32
AF = mybir.ActivationFunctionType
ALU = mybir.AluOpType
AX = mybir.AxisListType.X

NCORES = 8
D = 2048
FD = 16
QCD = 12
KVCD = 4
HN = 16
FF = 8192
NK = 4608
NKC = 36
NQ = 1024
NQT = 8
HPG = 2
NGRP = HN // HPG
SCALE = 1.0 / math.sqrt(128.0)
CB = 256.0
F16B = 1536.0
EPS = 1e-5
EXPB = -5.0
KB = 1024
SB0 = 16512
NW = float(D) * FF
DEBUG = False
STOP_AFTER = None


def build_nc():
    nc = bass.Bass("TRN2", target_bir_lowering=False)

    xk_d = nc.dram_tensor("xk", [NK, D], F32, kind="ExternalInput").ap()
    xo_d = nc.dram_tensor("xo", [NQ, D], F32, kind="ExternalInput").ap()
    wdq_d = nc.dram_tensor("wdqT", [D, 1536], F32, kind="ExternalInput").ap()
    wuq_d = nc.dram_tensor("wuqT", [1536, D], F32, kind="ExternalInput").ap()
    wdkv_d = nc.dram_tensor("wdkvT", [D, 512], F32, kind="ExternalInput").ap()
    wukv_d = nc.dram_tensor("wukvT", [512, 2 * D], F32, kind="ExternalInput").ap()
    wout_d = nc.dram_tensor("woutT", [D, D], F32, kind="ExternalInput").ap()
    wg_d = nc.dram_tensor("wgT", [D, FF], F32, kind="ExternalInput").ap()
    wu_d = nc.dram_tensor("wuT", [D, FF], F32, kind="ExternalInput").ap()
    wd_d = nc.dram_tensor("wdT", [FF, D], F32, kind="ExternalInput").ap()
    masks_d = nc.dram_tensor("masks", [512, 512], BF16, kind="ExternalInput").ap()
    ones_d = nc.dram_tensor("onesc", [128, 128], BF16, kind="ExternalInput").ap()
    ident_d = nc.dram_tensor("identc", [128, 128], BF16, kind="ExternalInput").ap()
    y_d = nc.dram_tensor("y", [NQ, D], F32, kind="ExternalOutput").ap()
    hs_d = nc.dram_tensor("hs", [NQ, FF], F32).ap()
    aos_d = nc.dram_tensor("aos", [FD, 128, NQ], BF16).ap()
    xqs_d = nc.dram_tensor("xqs", [FD, 128, NQ], BF16).ap()

    AT = nc.alloc_sbuf_tensor_at
    identb = AT("identb", [128, 128], BF16, offset=SB0 + 0)
    onesb = AT("onesb", [128, 128], BF16, offset=SB0 + 256)
    masksb = AT("masksb", [128, 4, 512], BF16, offset=SB0 + 512)
    off = SB0 + 4608
    def nx(shape, dt, name):
        nonlocal off
        t = AT(name, shape, dt, offset=off)
        sz = 1
        for s in shape[1:]:
            sz *= s
        off += sz * (4 if dt in (F32, I32) else 2)
        off = (off + 31) & ~31
        return t
    sums_c = nx([128, 48], F32, "sums_c")
    sumsq_c = nx([128, 48], F32, "sumsq_c")
    lnt0 = nx([128, 48], F32, "lnt0")
    lnt1 = nx([128, 48], F32, "lnt1")
    lnt2 = nx([128, 48], F32, "lnt2")
    scq = nx([128, 8], F32, "scq")
    xnmax = nx([128, 8], F32, "xnmax")
    acc3 = nx([128, 4], F32, "acc3")
    accT = nx([128, 4], F32, "accT")
    thr = nx([128, 16], F32, "thr")
    sgs = nx([128, 8], F32, "sgs")
    hmax = nx([128, 8], F32, "hmax")
    dscl = nx([128, 8], F32, "dscl")
    hmaxa = nx([128, FD, 8], F32, "hmaxa")
    auxs = nx([128, 16], F32, "auxs")
    cpool = nx([128, 4], F32, "cpool")
    onesf = nx([128, 128], F32, "onesf")
    identf = nx([128, 128], F32, "identf")
    cids0 = nx([1, 16], I32, "cids0")
    assert off <= SB0 + 8 * KB, off
    nc.const_aps.register(F32, EPS, cpool.ap()[:, 0:1])
    nc.const_aps.register(F32, CB, cpool.ap()[:, 1:2])
    nc.const_aps.register(F32, EXPB, cpool.ap()[:, 2:3])
    nc.const_aps.register(F32, F16B, cpool.ap()[:, 3:4])

    ckv = AT("ckv", [128, KVCD, NK], BF16, offset=SB0 + 8 * KB)
    wukv = AT("wukv", [128, KVCD, 2 * D], BF16, offset=SB0 + 44 * KB)
    qfm = AT("qfm", [128, HN, NQ], BF16, offset=SB0 + 76 * KB)
    otall = AT("otall", [128, HN, NQ], BF16, offset=SB0 + 108 * KB)
    wdkv = AT("wdkv", [128, FD, 512], BF16, offset=SB0 + 108 * KB)
    xntb = AT("xntb", [128, 2, D], BF16, offset=SB0 + 140 * KB)
    xnf = AT("xnf", [128, 2, FD, 512], BF16, offset=SB0 + 148 * KB)
    statda = AT("statda", [128, D], F32, offset=SB0 + 124 * KB)
    xt0 = AT("xt0", [128, D], F32, offset=SB0 + 188 * KB)
    xt1 = AT("xt1", [128, D], F32, offset=SB0 + 196 * KB)
    xqt = AT("xqt", [128, NQT, D], BF16, offset=SB0 + 76 * KB)
    xnofm = AT("xnofm", [128, FD, NQ], BF16, offset=SB0 + 108 * KB)
    xnof = AT("xnof", [128, 2, D], F32, offset=SB0 + 148 * KB)
    qu_u = AT("qu_u", [128, D], F32, offset=SB0 + 164 * KB)
    qu_h = AT("qu_h", [128, D], F16, offset=SB0 + 172 * KB)
    qu_g = AT("qu_g", [128, D], F32, offset=SB0 + 180 * KB)
    wst_f = AT("wst_f", [128, 2, D], F32, offset=SB0 + 148 * KB)
    wst_b = AT("wst_b", [128, 2, FD, 128], BF16, offset=SB0 + 164 * KB)
    qcfm = AT("qcfm", [128, QCD, NQ], BF16, offset=SB0 + 172 * KB)
    kg = AT("kg", [128, HPG, NK], BF16, offset=SB0 + 140 * KB)
    vg = AT("vg", [128, NKC, HPG * 128], BF16, offset=SB0 + 158 * KB)
    ptile = AT("ptile", [128, 2, HPG, 512], BF16, offset=SB0 + 176 * KB)
    recip_s = AT("recip_s", [128, 1024], F32, offset=SB0 + 180 * KB)
    rep_s = AT("rep_s", [128, 2, 512], F32, offset=SB0 + 184 * KB)
    aostg = AT("aostg", [128, 2, NQ], BF16, offset=SB0 + 148 * KB)
    wof = AT("wof", [128, 2, D], F32, offset=SB0 + 188 * KB)
    wob = AT("wob", [128, 2, FD, 128], BF16, offset=SB0 + 140 * KB)
    gsq = AT("gsq", [128, 2, 4, 512], F32, offset=SB0 + 8 * KB)
    usq = AT("usq", [128, 2, 4, 512], F32, offset=SB0 + 24 * KB)
    tgq2 = AT("tgq2", [128, FD, 512], BF16, offset=SB0 + 40 * KB)
    tuq2 = AT("tuq2", [128, FD, 512], BF16, offset=SB0 + 56 * KB)
    tgq = AT("tgq", [128, FD, 512], BF16, offset=SB0 + 72 * KB)
    tuq = AT("tuq", [128, FD, 512], BF16, offset=SB0 + 88 * KB)
    xqfm = AT("xqfm", [128, FD, NQ], BF16, offset=SB0 + 104 * KB)
    hg = AT("hg", [128, NQT, 512], F32, offset=SB0 + 136 * KB)
    hstg = AT("hstg", [128, NQT, 512], F32, offset=SB0 + 152 * KB)
    cmp_a = AT("cmp_a", [128, 4, 512], BF16, offset=SB0 + 200 * KB)
    hqfm = AT("hqfm", [128, 64, NQ], BF16, offset=SB0 + 40 * KB)
    hbuf = AT("hbuf", [128, FF], F32, offset=SB0 + 168 * KB)
    q2_h = AT("q2_h", [128, D], F16, offset=SB0 + 8 * KB)
    q2_u = AT("q2_u", [128, D], F32, offset=SB0 + 28 * KB)
    hqtm = AT("hqtm", [128, D], BF16, offset=SB0 + 24 * KB)
    ffnb = AT("ffnb", [128, FD, NQ], BF16, offset=SB0 + 8 * KB)
    dslab = AT("dslab", [128, FD, 128], F32, offset=SB0 + 168 * KB)
    dcma = AT("dcma", [128, FD, 128], BF16, offset=SB0 + 176 * KB)
    tdq = AT("tdq", [128, 64, 128], BF16, offset=SB0 + 184 * KB)
    pa0 = AT("pa0", [128, D], F32, offset=SB0 + 76 * KB)
    pa1 = AT("pa1", [128, D], F32, offset=SB0 + 84 * KB)
    wdst = AT("wdst", [128, FD, 256], F32, offset=SB0 + 92 * KB)
    pc0 = AT("pc0", [128, D], F32, offset=SB0 + 180 * KB)
    pd0 = AT("pd0", [128, D], F32, offset=SB0 + 196 * KB)
    pe0 = AT("pe0", [128, D], F32, offset=SB0 + 108 * KB)
    pe1 = AT("pe1", [128, D], F32, offset=SB0 + 116 * KB)
    pt0 = AT("pt0", [128, D], F32, offset=SB0 + 158 * KB)
    pt1 = AT("pt1", [128, D], F32, offset=SB0 + 166 * KB)
    accTw = AT("accTw", [128, 3, 64], F32, offset=SB0 + 204 * KB)
    aofm = AT("aofm", [128, FD, NQ], BF16, offset=SB0 + 40 * KB)
    sumb = AT("sumb", [128, FD, NQ], BF16, offset=SB0 + 72 * KB)
    xtb = AT("xtb", [128, 2, D], F32, offset=SB0 + 104 * KB)
    ystg = AT("ystg", [128, 2, D], F32, offset=SB0 + 120 * KB)

    psb0 = nc.alloc_psum_tensor("psb0", [128, 512], F32)
    psb1 = nc.alloc_psum_tensor("psb1", [128, 512], F32)
    psp23 = nc.alloc_psum_tensor("psp23", [128, 1024], F32)
    psp45 = nc.alloc_psum_tensor("psp45", [128, 1024], F32)
    psb6 = nc.alloc_psum_tensor("psb6", [128, 512], F32)
    psb7 = nc.alloc_psum_tensor("psb7", [128, 512], F32)

    def bank(i):
        if i == 0:
            return psb0[:]
        if i == 1:
            return psb1[:]
        if i in (2, 3):
            return psp23[:, (i - 2) * 512:(i - 2) * 512 + 512]
        if i in (4, 5):
            return psp45[:, (i - 4) * 512:(i - 4) * 512 + 512]
        if i == 6:
            return psb6[:]
        return psb7[:]

    def bank2(i):
        return psp23[:] if i == 2 else psp45[:]

    cid_d = nc.dram_tensor("cidc", [1, 4], I32, kind="ExternalInput").ap()
    cids = cids0

    sem_cms, sems = [], {}
    for s in ("c0", "x0", "x1", "w0", "w1", "g0", "g1", "g2", "p0", "p1",
              "mT", "mV", "mS", "mG", "mY"):
        cm = nc.semaphore(s)
        sems[s] = cm.__enter__()
        sem_cms.append(cm)

    class _Dummy:
        def then_inc(self, *a, **k):
            return self

        def _wait_ge(self, *a, **k):
            return self

    class _NullGuard:
        def __enter__(self):
            return self

        def __exit__(self, *a):
            return False

    _DUM = _Dummy()
    _NG = _NullGuard()

    class EP:
        def __init__(self, real, on):
            self._r = real
            self._on = on

        def __getattr__(self, name):
            if self._on:
                return getattr(self._r, name)
            if name in ("If_eq", "If_ne", "If_lt"):
                return lambda *a, **k: _NG
            return lambda *a, **k: _DUM

    DMA_SEMS = ("c0", "x0", "x1", "w0", "w1", "g0", "g1", "g2", "p0", "p1")

    def emit(nc, ACTIVE, cnt):
        real = {"tensor": nc.tensor, "vector": nc.vector, "scalar": nc.scalar,
                "gpsimd": nc.gpsimd, "sync": nc.sync}
        T = EP(nc.tensor, ACTIVE == "tensor")
        V = EP(nc.vector, ACTIVE == "vector")
        S = EP(nc.scalar, ACTIVE == "scalar")
        G = EP(nc.gpsimd, ACTIVE == "gpsimd")
        SY = EP(nc.sync, ACTIVE == "sync")
        e2s = {id(T): "mT", id(V): "mV", id(S): "mS", id(G): "mG", id(SY): "mY"}
        grd = {"tensor": T, "vector": V, "scalar": S}.get(ACTIVE)

        def inc(instr, eng):
            s = e2s[id(eng)]
            cnt[s] += 1
            instr.then_inc(sems[s], 1)
            eng.wait_ge(sems[s], cnt[s])
            return cnt[s]

        def dinc(instr, q):
            cnt[q] += 16
            instr.then_inc(sems[q], 16)
            return cnt[q]

        def wt(eng, s, val):
            if val > 0:
                eng.wait_ge(sems[s], val)

        ENGS = [("mT", T), ("mV", V), ("mS", S), ("mG", G), ("mY", SY)]

        def barrier():
            for nm, _e in ENGS:
                cnt[nm] += 1
            me = EP.__getattribute__(
                {"tensor": T, "vector": V, "scalar": S, "gpsimd": G, "sync": SY}[ACTIVE],
                "_r") if False else None
            ap = {"tensor": T, "vector": V, "scalar": S, "gpsimd": G, "sync": SY}[ACTIVE]
            ap.nop().then_inc(sems["m" + {"tensor": "T", "vector": "V", "scalar": "S",
                                          "gpsimd": "G", "sync": "Y"}[ACTIVE]], 1)
            for nm, _e in ENGS:
                if nm != "m" + {"tensor": "T", "vector": "V", "scalar": "S",
                                "gpsimd": "G", "sync": "Y"}[ACTIVE]:
                    ap.wait_ge(sems[nm], cnt[nm])
            for dq in DMA_SEMS:
                if cnt[dq]:
                    ap.wait_ge(sems[dq], cnt[dq])

        class _EarlyStop(Exception):
            pass

        def maybe_stop(tag):
            if STOP_AFTER == tag:
                barrier()
                gd = G.dma_start(out=y_d[0:128, 0:128], in_=identb[:])
                dinc(gd, "g0")
                wt(G, "g0", cnt["g0"])
                wt(SY, "g0", cnt["g0"])
                raise _EarlyStop()

        for p in range(4):
            dinc(SY.dma_start(out=masksb[:, p, :], in_=masks_d[p * 128:(p + 1) * 128, :]), "c0")
        dinc(SY.dma_start(out=onesb[:], in_=ones_d), "c0")
        dinc(SY.dma_start(out=identb[:], in_=ident_d), "c0")
        cid_dd = dinc(SY.dma_start(out=cids[0:1, 0:4], in_=cid_d), "c0")
        wt(V, "c0", cnt["c0"])
        inc(V.memset(acc3[:], 0.0), V)
        inc(V.memset(cpool[:, 0:1], EPS), V)
        inc(V.memset(cpool[:, 1:2], CB), V)
        inc(V.memset(cpool[:, 2:3], EXPB), V)
        inc(V.memset(cpool[:, 3:4], F16B), V)
        inc(V.memset(onesf[:], 1.0), V)
        vcp = inc(V.tensor_copy(out=identf[:], in_=identb[:]), V)
        wt(S, "mV", vcp)

        maybe_stop("CONST")
        pre_dd = [dinc(SY.dma_start(out=xt0[:], in_=xk_d[0:128, :]), "x0"),
                  dinc(SY.dma_start(out=xt1[:], in_=xk_d[128:256, :]), "x1")]
        for h2 in range(2):
            sem = "w0" if h2 == 0 else "w1"
            if h2 == 1:
                wt(SY, "mV", cnt["mV"])
            dd = dinc(SY.dma_start(
                out=wdst[:],
                in_=wdkv_d[:, h2 * 256:(h2 + 1) * 256].rearrange("(f p) c -> p f c", p=128)), sem)
            wt(V, sem, dd)
            inc(V.tensor_copy(out=wdkv[:, :, h2 * 256:(h2 + 1) * 256], in_=wdst[:]), V)
        maybe_stop("WDKV")

        def ln_stats(buf, col, dmy=None):
            dmy = statda if dmy is None else dmy
            a = S.activation(out=dmy[:], in_=buf[:], func=AF.Square,
                             accum_out=sumsq_c[:, col:col + 1])
            inc(a, S)
            a = S.activation(out=dmy[:], in_=buf[:], func=AF.Identity,
                             accum_out=sums_c[:, col:col + 1])
            return inc(a, S)

        def ln_norm(buf, col, st_sm, own_idx=None):
            wt(V, "mS", st_sm)
            inc(V.tensor_scalar(out=lnt0[:, col:col + 1], in0=sums_c[:, col:col + 1],
                                scalar1=1.0 / D, scalar2=None, op0=ALU.mult), V)
            inc(V.scalar_tensor_tensor(out=lnt1[:, col:col + 1], in0=lnt0[:, col:col + 1],
                                       scalar=1.0, in1=lnt0[:, col:col + 1],
                                       op0=ALU.mult, op1=ALU.mult), V)
            vm = inc(V.scalar_tensor_tensor(out=lnt1[:, col:col + 1], in0=sumsq_c[:, col:col + 1],
                                            scalar=1.0 / D, in1=lnt1[:, col:col + 1],
                                            op0=ALU.mult, op1=ALU.subtract), V)
            wt(S, "mV", vm)
            inc(S.activation(out=lnt1[:, col:col + 1], in_=lnt1[:, col:col + 1],
                             func=AF.Ln, bias=EPS), S)
            sm = inc(S.activation(out=lnt1[:, col:col + 1], in_=lnt1[:, col:col + 1],
                                  func=AF.Exp, scale=-0.5), S)
            wt(V, "mS", sm)
            vm = inc(V.scalar_tensor_tensor(out=lnt2[:, col:col + 1], in0=lnt0[:, col:col + 1],
                                            scalar=-1.0, in1=lnt1[:, col:col + 1],
                                            op0=ALU.mult, op1=ALU.mult), V)
            wt(S, "mV", vm)
            sm = inc(S.activation(out=xntb[:, col % 2, :], in_=buf[:], func=AF.Identity,
                                  scale=lnt1[:, col:col + 1], bias=lnt2[:, col:col + 1]), S)
            if own_idx is not None:
                sm = inc(S.activation(out=xnof[:, own_idx % 2, :], in_=buf[:], func=AF.Identity,
                                      scale=lnt1[:, col:col + 1], bias=lnt2[:, col:col + 1]), S)
            return sm

        tb_ev = {4: 0, 5: 0, 6: 0, 7: 0}

        def transpose_tile(src_tile, dst_fn, wait_s=0, wait_v=0):
            if wait_s:
                wt(T, "mS", wait_s)
            if wait_v:
                wt(T, "mV", wait_v)
            for fc in range(FD):
                b = 4 + (fc % 4)
                wt(T, "mV", tb_ev[b])
                mm = T.matmul(out=bank(b)[:, 0:128],
                              lhsT=src_tile[:, fc * 128:(fc + 1) * 128],
                              rhs=identb[:], start=True, stop=True)
                tm = inc(mm, T)
                wt(V, "mT", tm)
                ev = V.tensor_copy(out=dst_fn(fc), in_=bank(b)[:, 0:128])
                tb_ev[b] = inc(ev, V)

        scale_blocks = []
        for mi, wsrc in enumerate((wg_d, wu_d, wd_d)):
            for rb in range(wsrc.shape[0] // 128):
                for sb in range(wsrc.shape[1] // D):
                    scale_blocks.append((wsrc, rb, sb, mi))
        sc_i = {"v": 0}
        sc_done = {}

        def pump_scale(n, bufs=None, on_s=False, dq=None, psem="p0"):
            if bufs is None:
                bufs = (xt0, xt1)
            if dq is None:
                dq = G
            for _ in range(n):
                k = sc_i["v"]
                if k >= len(scale_blocks):
                    return
                sc_i["v"] += 1
                wsrc, rb, sb, mi = scale_blocks[k]
                slot = k % len(bufs)
                sbuf = bufs[slot]
                ps = psem if len(bufs) == 1 else ("p0" if slot == 0 else "p1")
                col = k - 64 * mi
                prev = sc_done.get(id(sbuf))
                if prev:
                    wt(dq, prev[0], prev[1])
                lastp = sc_done.get(ps)
                if lastp and lastp != prev:
                    wt(dq, lastp[0], lastp[1])
                dd = dinc(dq.dma_start(out=sbuf[:],
                                       in_=wsrc[rb * 128:(rb + 1) * 128, sb * D:(sb + 1) * D]),
                          ps)
                if on_s:
                    wt(S, ps, dd)
                    sm = inc(S.activation(out=sbuf[:], in_=sbuf[:], func=AF.Abs,
                                          accum_out=accTw[:, mi, col:col + 1]), S)
                    sc_done[id(sbuf)] = sc_done[ps] = ("mS", sm)
                else:
                    wt(V, ps, dd)
                    vm = inc(V.reduce_sum(out=accTw[:, mi, col:col + 1], in_=sbuf[:], axis=AX,
                                          apply_absolute_value=True), V)
                    sc_done[id(sbuf)] = sc_done[ps] = ("mV", vm)

        xn_sm = [0] * (NKC + NQT)
        tp_done = [0] * (NKC + NQT)
        dkv_done = [0] * 12
        st_sm = [0] * (NKC + NQT)
        dds = [0] * NKC
        dds[0], dds[1] = pre_dd
        wt(S, "x0", dds[0])
        st_sm[0] = ln_stats(xt0, 0)
        for it in range(NKC):
            buf = xt0 if it % 2 == 0 else xt1
            if it + 1 < NKC:
                nbuf = xt1 if it % 2 == 0 else xt0
                nsem = "x1" if it % 2 == 0 else "x0"
                if it + 1 >= 2:
                    wt(SY, "mS", xn_sm[it - 1])
                    dds[it + 1] = dinc(SY.dma_start(out=nbuf[:],
                                                    in_=xk_d[(it + 1) * 128:(it + 2) * 128, :]), nsem)
                wt(S, nsem, dds[it + 1])
                st_sm[it + 1] = ln_stats(nbuf, it + 1)
            if it >= 2:
                wt(S, "mV", tp_done[it - 2])
            xn_sm[it] = ln_norm(buf, it, st_sm[it])
            gi, sl = it // 4, it % 4

            def dstf(fc, gi=gi, sl=sl):
                return xnf[:, gi % 2, fc, sl * 128:(sl + 1) * 128]

            wv = dkv_done[gi - 2] if gi >= 2 and sl == 0 else 0
            transpose_tile(xntb[:, it % 2, :], dstf, wait_s=xn_sm[it], wait_v=wv)
            tp_done[it] = cnt["mV"]
            if sl == 3:
                wt(T, "mV", tp_done[it])
                for cc in range(KVCD):
                    for fc in range(FD):
                        mm = T.matmul(out=bank(cc), lhsT=wdkv[:, fc, cc * 128:(cc + 1) * 128],
                                      rhs=xnf[:, gi % 2, fc, :],
                                      start=(fc == 0), stop=(fc == FD - 1))
                    tm = inc(mm, T)
                    wt(V, "mT", tm)
                    ev = V.tensor_copy(out=ckv[:, cc, gi * 512:(gi + 1) * 512], in_=bank(cc))
                    dkv_done[gi] = inc(ev, V)
            pump_scale(1, (pa0, pa1))
        barrier()

        maybe_stop('P0a')
        xq_vm = [0] * NQT
        bdd = [0] * NQT
        bdd[0] = dinc(SY.dma_start(out=xt0[:], in_=xo_d[0:128, :]), "x0")
        wt(S, "x0", bdd[0])
        st_sm[NKC] = ln_stats(xt0, NKC, dmy=qu_g)
        for it in range(NQT):
            buf = xt0 if it % 2 == 0 else xt1
            if it + 1 < NQT:
                nbuf = xt1 if it % 2 == 0 else xt0
                nsem = "x1" if it % 2 == 0 else "x0"
                if it + 1 >= 2:
                    wt(SY, "mS", xn_sm[NKC + it - 1])
                bdd[it + 1] = dinc(SY.dma_start(out=nbuf[:],
                                                in_=xo_d[(it + 1) * 128:(it + 2) * 128, :]), nsem)
                wt(S, nsem, bdd[it + 1])
                st_sm[NKC + it + 1] = ln_stats(nbuf, NKC + it + 1, dmy=qu_g)
            if it >= 2:
                wt(S, "mV", tp_done[NKC + it - 2])
            if it >= 1:
                wt(S, "mV", xq_vm[it - 1])
            xn_sm[NKC + it] = ln_norm(buf, NKC + it, st_sm[NKC + it], own_idx=it)
            sm = xn_sm[NKC + it]
            wt(V, "mS", sm)
            inc(V.reduce_max(out=xnmax[:, it:it + 1], in_=xnof[:, it % 2, :], axis=AX,
                             apply_absolute_value=True), V)
            inc(V.tensor_scalar(out=scq[:, it:it + 1], in0=xnmax[:, it:it + 1],
                                scalar1=1.0 / 127.0, scalar2=1e-5 / 127.0,
                                op0=ALU.mult, op1=ALU.max), V)
            vm = inc(V.reciprocal(out=scq[:, it:it + 1], in_=scq[:, it:it + 1]), V)
            wt(S, "mV", vm)
            a = S.activation(out=qu_u[:], in_=xnof[:, it % 2, :], func=AF.Identity,
                             scale=scq[:, it:it + 1], bias=F16B)
            sm2 = inc(a, S)
            wt(V, "mS", sm2)
            inc(V.tensor_copy(out=qu_h[:], in_=qu_u[:]), V)
            xq_vm[it] = inc(V.tensor_scalar(out=xqt[:, it, :], in0=qu_h[:], scalar1=-F16B,
                                            scalar2=None, op0=ALU.add), V)

            def dsto(fc, it=it):
                return xnofm[:, fc, it * 128:(it + 1) * 128]

            transpose_tile(xntb[:, (NKC + it) % 2, :], dsto, wait_s=sm)
            tp_done[NKC + it] = cnt["mV"]
        barrier()

        maybe_stop('P0b')
        wukv_dd = 0
        for cc in range(KVCD):
            wukv_dd = dinc(G.dma_start(out=wukv[:, cc, :],
                                       in_=wukv_d[cc * 128:(cc + 1) * 128, :]), "g2")
        xq_dg = [0] * NQT
        for it in range(NQT):
            def dstq(fc, it=it):
                return xnf[:, it % 2, fc, 0:128]

            wv = xq_dg[it - 2] if it >= 2 else 0
            if wv:
                wt(V, "g0" if it % 2 == 0 else "g1", wv)
            transpose_tile(xqt[:, it, :], dstq, wait_v=xq_vm[it])
            vm = cnt["mV"]
            wt(G, "mV", vm)
            gd = G.dma_start(out=xqs_d[:, :, it * 128:(it + 1) * 128].rearrange("f p c -> p f c"),
                             in_=xnf[:, it % 2, :, 0:128])
            xq_dg[it] = dinc(gd, "g0" if it % 2 == 0 else "g1")
            pump_scale(1, (pc0,), on_s=True)
        barrier()

        maybe_stop('P0c')
        wcast_done = {0: 0, 1: 0}

        def stream_blocks(src, nchunks, col0, half, pre=None):
            if pre is None:
                if wcast_done[half]:
                    wt(SY, "mV", wcast_done[half])
                dd = dinc(SY.dma_start(
                    out=wst_f[:, half, 0:nchunks * 128].rearrange("p (f c) -> p f c", c=128),
                    in_=src[0:nchunks * 128, col0:col0 + 128].rearrange("(f p) c -> p f c", p=128)),
                    "w0" if half == 0 else "w1")
            else:
                dd = pre
            wt(V, "w0" if half == 0 else "w1", dd)
            wcast_done[half] = inc(
                V.tensor_copy(out=wst_b[:, half, 0:nchunks, :].rearrange("p f c -> p (f c)"),
                              in_=wst_f[:, half, 0:nchunks * 128]), V)
            return wcast_done[half]

        ev_bank = {}
        wv_next = stream_blocks(wdq_d, FD, 0, 0)
        for qc in range(QCD):
            wv = wv_next
            if qc + 1 < QCD:
                wv_next = stream_blocks(wdq_d, FD, (qc + 1) * 128, (qc + 1) % 2)
            wt(T, "mV", wv)
            for half in range(2):
                b = (qc * 2 + half) % 4
                wt(T, "mV", ev_bank.get(b, 0))
                for fc in range(FD):
                    mm = T.matmul(out=bank(b), lhsT=wst_b[:, qc % 2, fc, :],
                                  rhs=xnofm[:, fc, half * 512:(half + 1) * 512],
                                  start=(fc == 0), stop=(fc == FD - 1))
                tm = inc(mm, T)
                wt(V, "mT", tm)
                ev = V.tensor_copy(out=qcfm[:, qc, half * 512:(half + 1) * 512], in_=bank(b))
                ev_bank[b] = inc(ev, V)
            pump_scale(1, (pd0,), on_s=True)
            if qc == QCD - 2:
                wt(SY, "mV", wcast_done[0])
                uq0_dd = dinc(SY.dma_start(
                    out=wst_f[:, 0, 0:QCD * 128].rearrange("p (f c) -> p f c", c=128),
                    in_=wuq_d[0:QCD * 128, 0:128].rearrange("(f p) c -> p f c", p=128)), "w0")
        wt(G, "mT", cnt["mT"])
        wv_next = stream_blocks(wuq_d, QCD, 0, 0, pre=uq0_dd)
        for oc in range(FD):
            wv = wv_next
            if oc + 1 < FD:
                wv_next = stream_blocks(wuq_d, QCD, (oc + 1) * 128, (oc + 1) % 2)
            wt(T, "mV", wv)
            for half in range(2):
                b = (oc * 2 + half) % 4
                wt(T, "mV", ev_bank.get(b, 0))
                for qc in range(QCD):
                    mm = T.matmul(out=bank(b), lhsT=wst_b[:, oc % 2, qc, :],
                                  rhs=qcfm[:, qc, half * 512:(half + 1) * 512],
                                  start=(qc == 0), stop=(qc == QCD - 1))
                tm = inc(mm, T)
                wt(V, "mT", tm)
                ev = V.tensor_copy(out=qfm[:, oc, half * 512:(half + 1) * 512], in_=bank(b))
                ev_bank[b] = inc(ev, V)
            pump_scale(2, (pe0, pe1), on_s=True)
        p0e_T = cnt["mT"]
        p0e_V = cnt["mV"]
        p0e_S = cnt["mS"]

        maybe_stop('P0de')
        cid_reg = grd.alloc_register("cidreg") if grd else None
        if grd:
            wt(grd, "c0", cid_dd)
            grd.reg_load(cid_reg, cids[0:1, 0:1])

        def emit_attn_sweep(cid, h0, kv_ready, fin_in=0, gate_s=0):
            nA = 4 * (cid + 1)
            half_first = {0: 0, 1: nA}
            half_last = {0: nA - 1, 1: NKC - 1}
            t0, v0, s0 = cnt["mT"], cnt["mV"], cnt["mS"]
            jlo = 0
            wt(T, "mV", kv_ready)
            wt(T, "mV", fin_in)
            wt(V, "mS", gate_s)
            exp_sm = {}
            post_tm = {}

            def boundary(half):
                wt(V, "mT", cnt["mT"])
                inc(V.reciprocal(out=recip_s[0:64, 0:512], in_=psb6[0:64, :]), V)
                vm1 = inc(V.reciprocal(out=recip_s[0:64, 512:1024], in_=psb7[0:64, :]), V)
                fin = 0
                for hh in range(HPG):
                    wt(T, "mV", vm1 if hh == 0 else fin)
                    mm = T.matmul(out=bank(7), lhsT=onesf[0:1, :],
                                  rhs=recip_s[0:1, 512 * hh:512 * hh + 512],
                                  start=True, stop=True)
                    tmr = inc(mm, T)
                    wt(V, "mT", tmr)
                    inc(V.tensor_copy(out=rep_s[:, hh, :], in_=bank(7)), V)
                    ev = V.tensor_tensor(out=otall[:, h0 + hh, half * 512:(half + 1) * 512],
                                         in0=bank(hh), in1=rep_s[:, hh, :], op=ALU.mult)
                    fin = inc(ev, V)
                return fin

            fin_vm = 0
            nb = 0
            nmask = 0

            def emit_scores(j):
                half = 0 if j < nA else 1
                qc0 = half * 512
                sp = 2 + 2 * ((j - jlo) % 2)
                if (j - jlo) >= 2:
                    wt(T, "mS", exp_sm[j - 2])
                for hh in range(HPG):
                    mm = T.matmul(out=bank(sp + hh), lhsT=kg[:, hh, (j - jlo) * 128:(j - jlo + 1) * 128],
                                  rhs=qfm[:, h0 + hh, qc0:qc0 + 512], start=True, stop=True)
                tm = inc(mm, T)
                wt(S, "mT", tm)
                if (j - jlo) >= 2:
                    wt(S, "mT", post_tm[j - 2])
                pb = (j - jlo) % 2
                a = S.activation(out=ptile[:, pb, :, :].rearrange("p h c -> p (h c)"),
                                 in_=bank2(sp), func=AF.Exp, scale=SCALE, bias=EXPB)
                exp_sm[j] = inc(a, S)

            emit_scores(0)
            for j in range(NKC):
                half = 0 if j < nA else 1
                pb = (j - jlo) % 2
                if j + 1 < NKC and j + 1 != nA:
                    emit_scores(j + 1)
                if half == 0:
                    mp = j - (nA - 4)
                else:
                    mp = j - 32
                use_mask = 0 <= mp < 4
                if use_mask:
                    nmask += 1
                    wt(V, "mS", exp_sm[j])
                    inc(V.tensor_tensor(out=ptile[:, pb, 0, :], in0=ptile[:, pb, 0, :],
                                        in1=masksb[:, mp, :], op=ALU.mult), V)
                    pvm = inc(V.tensor_tensor(out=ptile[:, pb, 1, :], in0=ptile[:, pb, 1, :],
                                              in1=masksb[:, mp, :], op=ALU.mult), V)
                    wt(T, "mV", pvm)
                else:
                    wt(T, "mS", exp_sm[j])
                if fin_vm and j == nA:
                    wt(T, "mV", fin_vm)
                st = (j == half_first[half])
                sp_ = (j == half_last[half])
                for hh in range(HPG):
                    T.matmul(out=(psb6 if hh == 0 else psb7)[0:64, :],
                             lhsT=onesb[:, 0:64], rhs=ptile[:, pb, hh, :],
                             start=st, stop=sp_)
                for hh in range(HPG):
                    mm = T.matmul(out=bank(hh), lhsT=vg[:, j - jlo, hh * 128:(hh + 1) * 128],
                                  rhs=ptile[:, pb, hh, :], start=st, stop=sp_)
                post_tm[j] = inc(mm, T)
                if sp_:
                    fin_vm = boundary(half)
                    nb += 1
                    if j + 1 < NKC and j + 1 == nA:
                        emit_scores(j + 1)
            maxT = 2 * NKC + 4
            maxV = 16 + 12
            maxS = NKC
            dT = (t0 + maxT) - cnt["mT"]
            dV = (v0 + maxV) - cnt["mV"]
            dS = (s0 + maxS) - cnt["mS"]
            assert dT >= 0 and dV >= 0 and dS >= 0, (cid, dT, dV, dS)
            for _ in range(dT):
                inc(T.nop(), T)
            for _ in range(dV):
                inc(V.nop(), V)
            for _ in range(dS):
                inc(S.nop(), S)
            return fin_vm

        fin_prev = {c: 0 for c in range(NCORES)}
        ms_sweeps = 0
        for g in range(NGRP):
            h0 = g * HPG
            if ms_sweeps:
                wt(T, "mS", ms_sweeps)
            if g == 0:
                kv_ev = {2: ev_bank.get(2, 0), 3: ev_bank.get(3, 0),
                         4: tb_ev[4], 5: tb_ev[5]}
                wt(T, "mV", p0e_V)
                wt(T, "g2", wukv_dd)
            else:
                kv_ev = {}
            evi = 0
            for kt in range(9):
                k0 = kt * 512
                for hh in range(HPG):
                    b = 2 + (2 * kt + hh) % 4
                    wt(T, "mV", kv_ev.get(b, 0))
                    for cc in range(KVCD):
                        mm = T.matmul(out=bank(b),
                                      lhsT=wukv[:, cc, (h0 + hh) * 128:(h0 + hh + 1) * 128],
                                      rhs=ckv[:, cc, k0:k0 + 512],
                                      start=(cc == 0), stop=(cc == KVCD - 1))
                    tm = inc(mm, T)
                    wt(V, "mT", tm)
                    ev = V.tensor_copy(out=kg[:, hh, k0:k0 + 512], in_=bank(b))
                    kv_ev[b] = inc(ev, V)
                    evi += 1
                    if g > 0 and evi % 4 == 0:
                        pump_scale(1)
                    if g > 0 and evi in (2, 6):
                        pump_scale(1)
            for jj in range(NKC):
                b = 2 + (jj % 4)
                wt(T, "mV", kv_ev.get(b, 0))
                for cc in range(KVCD):
                    mm = T.matmul(out=bank(b)[:, 0:HPG * 128],
                                  lhsT=ckv[:, cc, jj * 128:(jj + 1) * 128],
                                  rhs=wukv[:, cc, 2048 + h0 * 128:2048 + (h0 + HPG) * 128],
                                  start=(cc == 0), stop=(cc == KVCD - 1))
                tm = inc(mm, T)
                wt(V, "mT", tm)
                ev = V.tensor_copy(out=vg[:, jj, :], in_=bank(b)[:, 0:HPG * 128])
                kv_ev[b] = inc(ev, V)
                evi += 1
                if g > 0 and evi % 4 == 0 and evi <= 52:
                    pump_scale(1)
            kv_ready = cnt["mV"]

            base_cnt = dict(cnt)
            finals = []
            fin_out = {}
            for cid in range(NCORES):
                for kk in cnt:
                    cnt[kk] = base_cnt[kk]
                with (grd.If_eq(cid_reg, cid) if grd else _NG):
                    fin_out[cid] = emit_attn_sweep(cid, h0, kv_ready, fin_prev[cid],
                                                   gate_s=p0e_S if g == 0 else 0)
                finals.append(dict(cnt))
            for f in finals:
                assert f == finals[0], (finals[0], f)
            for kk in cnt:
                cnt[kk] = finals[0][kk]
            fin_prev = fin_out
            ms_sweeps = cnt["mS"]
            if g == 0:
                wt(G, "mS", p0e_S)
                wt(G, "mV", p0e_V)
            maybe_stop("G%d" % g)
        barrier()

        for b_ in (xt0, xt1):
            pv = sc_done.get(id(b_))
            if pv:
                wt(SY, pv[0], pv[1])
        aos_dg = [0] * FD
        wcast_done = {0: 0, 1: 0}

        def stream_wout(oc):
            half = oc % 2
            if wcast_done[half]:
                wt(SY, "mV", wcast_done[half])
            dd = dinc(SY.dma_start(
                out=wof[:, half, :].rearrange("p (f c) -> p f c", c=128),
                in_=wout_d[:, oc * 128:(oc + 1) * 128].rearrange("(f p) c -> p f c", p=128)),
                "w0" if half == 0 else "w1")
            wt(V, "w0" if half == 0 else "w1", dd)
            wcast_done[half] = inc(
                V.tensor_copy(out=wob[:, half, :, :].rearrange("p f c -> p (f c)"),
                              in_=wof[:, half, :]), V)
            return wcast_done[half]

        wv_next = stream_wout(0)
        for oc in range(FD):
            wv = wv_next
            if oc + 1 < FD:
                wv_next = stream_wout(oc + 1)
            wt(T, "mV", wv)
            for half in range(2):
                b = (oc * 2 + half) % 4
                wt(T, "mV", ev_bank.get(b, 0))
                for ic in range(FD):
                    mm = T.matmul(out=bank(b), lhsT=wob[:, oc % 2, ic, :],
                                  rhs=otall[:, ic, half * 512:(half + 1) * 512],
                                  start=(ic == 0), stop=(ic == FD - 1))
                tm = inc(mm, T)
                wt(V, "mT", tm)
                if oc >= 2 and half == 0:
                    wt(V, "g0" if oc % 2 == 0 else "g1", aos_dg[oc - 2])
                ev = V.tensor_copy(out=aostg[:, oc % 2, half * 512:(half + 1) * 512], in_=bank(b))
                ev_bank[b] = inc(ev, V)
            wt(G, "mV", cnt["mV"])
            gd = G.dma_start(out=aos_d[oc], in_=aostg[:, oc % 2, :])
            aos_dg[oc] = dinc(gd, "g0" if oc % 2 == 0 else "g1")
        wt(SY, "mT", cnt["mT"])
        wt(SY, "mV", wcast_done[0])
        wt(SY, "mV", wcast_done[1])
        gsrc = wg_d.rearrange("(fc p) (hc c) -> fc p hc c", p=128, c=512)
        usrc = wu_d.rearrange("(fc p) (hc c) -> fc p hc c", p=128, c=512)
        p2a_xq = dinc(SY.dma_start(out=xqfm[:], in_=xqs_d.rearrange("f p c -> p f c")), "c0")
        p2a_g0 = dinc(SY.dma_start(out=gsq[:, 0, :, :],
                                   in_=gsrc[0:4, :, 0, :].rearrange("f p c -> p f c")), "w0")
        p2a_u0 = dinc(SY.dma_start(out=usq[:, 0, :, :],
                                   in_=usrc[0:4, :, 0, :].rearrange("f p c -> p f c")), "x0")
        wt(G, "mT", cnt["mT"])
        pump_scale(10 ** 9, bufs=(pt0, pt1), on_s=True)

        maybe_stop('P1out')
        wt(V, "mS", cnt["mS"])
        for mi in range(3):
            inc(V.reduce_sum(out=acc3[:, mi:mi + 1], in_=accTw[:, mi, :], axis=AX), V)
        vm = inc(V.tensor_copy(out=auxs[:, 0:4], in_=acc3[:, 0:4]), V)
        wt(T, "mV", vm)
        tm = inc(T.matmul(out=bank(0)[0:1, 0:4], lhsT=onesf[:, 0:1], rhs=auxs[:, 0:4],
                          start=True, stop=True), T)
        wt(V, "mT", tm)
        vm = inc(V.tensor_copy(out=auxs[0:1, 4:8], in_=bank(0)[0:1, 0:4]), V)
        wt(T, "mV", vm)
        tm = inc(T.matmul(out=bank(1)[:, 0:4], lhsT=onesf[0:1, :], rhs=auxs[0:1, 4:8],
                          start=True, stop=True), T)
        wt(V, "mT", tm)
        inc(V.tensor_copy(out=auxs[:, 8:12], in_=bank(1)[:, 0:4]), V)
        for (col, src, mul) in ((0, 0, 0.5 / NW), (1, 0, -0.5 / NW), (2, 1, 0.5 / NW),
                                (3, 1, -0.5 / NW), (4, 2, 0.5 / NW), (5, 2, -0.5 / NW),
                                (6, 0, 1.0 / (NW * 127.0))):
            inc(V.tensor_scalar(out=thr[:, col:col + 1], in0=auxs[:, 8 + src:9 + src],
                                scalar1=mul, scalar2=None, op0=ALU.mult), V)
        inc(V.scalar_tensor_tensor(out=thr[:, 7:8], in0=auxs[:, 10:11],
                                   scalar=1.0 / (NW * NW * 127.0 * 127.0),
                                   in1=auxs[:, 9:10], op0=ALU.mult, op1=ALU.mult), V)
        inc(V.tensor_scalar(out=sgs[:], in0=xnmax[:], scalar1=thr[:, 6:7],
                            scalar2=None, op0=ALU.mult), V)

        maybe_stop('SCL')
        wt(T, "c0", p2a_xq)
        wt(S, "g0", aos_dg[FD - 2])
        wt(S, "g1", aos_dg[FD - 1])

        hs_dg = [0] * FD
        tq_used = {0: 0, 1: 0}
        sq_done = {}
        silu_sm = [0] * NQT
        ev_tb = [0] * NQT
        pend = None
        for hc in range(FD):
            setq = hc % 2
            tg = tgq if setq == 0 else tgq2
            tu = tuq if setq == 0 else tuq2
            qg_q = [0] * 4
            qu_q = [0] * 4
            for q in range(4):
                gi = (hc * 4 + q) % 2
                gsem = "w0" if gi == 0 else "w1"
                usem = "x0" if gi == 0 else "x1"
                if hc == 0 and q == 0:
                    ddg, ddu = p2a_g0, p2a_u0
                else:
                    if sq_done.get(("g", gi)):
                        wt(SY, "mV", sq_done[("g", gi)])
                    ddg = dinc(SY.dma_start(out=gsq[:, gi, :, :],
                                            in_=gsrc[4 * q:4 * q + 4, :, hc, :].rearrange("f p c -> p f c")), gsem)
                    if sq_done.get(("u", gi)):
                        wt(SY, "mV", sq_done[("u", gi)])
                    ddu = dinc(SY.dma_start(out=usq[:, gi, :, :],
                                            in_=usrc[4 * q:4 * q + 4, :, hc, :].rearrange("f p c -> p f c")), usem)
                wt(V, gsem, ddg)
                if q == 0:
                    wt(V, "mT", tq_used[setq])
                fl = gsq[:, gi, :, :].rearrange("p f c -> p (f c)")
                fa = cmp_a[:, 0:4, :].rearrange("p f c -> p (f c)")
                inc(V.tensor_scalar(out=fa, in0=fl, scalar1=thr[:, 1:2],
                                    scalar2=None, op0=ALU.is_lt), V)
                fqg = tg[:, 4 * q:4 * q + 4, :].rearrange("p f c -> p (f c)")
                qg_q[q] = sq_done[("g", gi)] = inc(V.scalar_tensor_tensor(
                    out=fqg, in0=fl, scalar=thr[:, 0:1], in1=fa,
                    op0=ALU.is_gt, op1=ALU.subtract), V)
                wt(V, usem, ddu)
                flu = usq[:, gi, :, :].rearrange("p f c -> p (f c)")
                inc(V.tensor_scalar(out=fa, in0=flu, scalar1=thr[:, 3:4],
                                    scalar2=None, op0=ALU.is_lt), V)
                fqu = tu[:, 4 * q:4 * q + 4, :].rearrange("p f c -> p (f c)")
                qu_q[q] = sq_done[("u", gi)] = inc(V.scalar_tensor_tensor(
                    out=fqu, in0=flu, scalar=thr[:, 2:3], in1=fa,
                    op0=ALU.is_gt, op1=ALU.subtract), V)
            if pend is not None:
                ehc, etmu = pend
                for tb in range(NQT):
                    wt(V, "mT", etmu[tb])
                    if ehc >= 1 and tb == 0:
                        wt(V, "g2", hs_dg[ehc - 1])
                    inc(V.tensor_tensor(out=hstg[:, tb, :], in0=bank(tb), in1=hg[:, tb, :],
                                        op=ALU.mult), V)
                    ev_tb[tb] = cnt["mV"]
                    inc(V.reduce_max(out=hmaxa[:, ehc, tb:tb + 1], in_=hstg[:, tb, :], axis=AX,
                                     apply_absolute_value=True), V)
                wt(G, "mV", cnt["mV"])
                gd = G.dma_start(out=hs_d.rearrange("(tb p) (hc c) -> tb p hc c", p=128, c=512)[:, :, ehc, :].rearrange("tb p c -> p tb c"),
                                 in_=hstg[:])
                hs_dg[ehc] = dinc(gd, "g2")
            tmg = [0] * NQT
            for q in range(4):
                wt(T, "mV", qg_q[q])
                for f4 in range(4):
                    fc = 4 * q + f4
                    for tb in range(NQT):
                        if fc == 0 and hc >= 1:
                            wt(T, "mV", ev_tb[tb])
                        mm = T.matmul(out=bank(tb), lhsT=xqfm[:, fc, tb * 128:(tb + 1) * 128],
                                      rhs=tg[:, fc, :], start=(fc == 0), stop=(fc == FD - 1))
                        if fc == FD - 1:
                            tmg[tb] = inc(mm, T)
            for tb in range(NQT):
                wt(S, "mT", tmg[tb])
                silu_sm[tb] = inc(S.activation(out=hg[:, tb, :], in_=bank(tb), func=AF.Silu,
                                               scale=sgs[:, tb:tb + 1]), S)
            tmu = [0] * NQT
            for q in range(4):
                wt(T, "mV", qu_q[q])
                for f4 in range(4):
                    fc = 4 * q + f4
                    for tb in range(NQT):
                        if fc == 0:
                            wt(T, "mS", silu_sm[tb])
                        mm = T.matmul(out=bank(tb), lhsT=xqfm[:, fc, tb * 128:(tb + 1) * 128],
                                      rhs=tu[:, fc, :], start=(fc == 0), stop=(fc == FD - 1))
                        if fc == FD - 1:
                            tmu[tb] = inc(mm, T)
            tq_used[setq] = cnt["mT"]
            pend = (hc, list(tmu))
        ehc, etmu = pend
        for tb in range(NQT):
            wt(V, "mT", etmu[tb])
            if tb == 0:
                wt(V, "g2", hs_dg[ehc - 1])
            inc(V.tensor_tensor(out=hstg[:, tb, :], in0=bank(tb), in1=hg[:, tb, :],
                                op=ALU.mult), V)
            inc(V.reduce_max(out=hmaxa[:, ehc, tb:tb + 1], in_=hstg[:, tb, :], axis=AX,
                             apply_absolute_value=True), V)
        wt(G, "mV", cnt["mV"])
        gd = G.dma_start(out=hs_d.rearrange("(tb p) (hc c) -> tb p hc c", p=128, c=512)[:, :, ehc, :].rearrange("tb p c -> p tb c"),
                         in_=hstg[:])
        hs_dg[ehc] = dinc(gd, "g2")
        for b_ in (4, 5, 6, 7):
            tb_ev[b_] = cnt["mV"]

        maybe_stop('P2a')
        s_last = [0] * NQT
        act_q = [0, 0, 0, 0]
        dqd = [0, 0, 0, 0]
        QS = ("w0", "w1", "x0", "x1")
        for vv in sorted(sq_done.values()):
            wt(SY, "mV", vv)
        for tb in range(NQT):
            for qt in range(4):
                if tb == 0:
                    wt(SY, "g2", hs_dg[4 * qt + 3])
                if tb >= 1:
                    wt(SY, "mS", act_q[qt])
                dqd[qt] = dinc(SY.dma_start(out=hbuf[:, qt * D:(qt + 1) * D],
                                            in_=hs_d[tb * 128:(tb + 1) * 128, qt * D:(qt + 1) * D]),
                               QS[qt])
            if tb >= 1:
                wt(V, "mS", s_last[tb - 1])
            inc(V.reduce_max(out=hmax[:, tb:tb + 1],
                             in_=hmaxa[:, :, tb:tb + 1].rearrange("p f c -> p (f c)"),
                             axis=AX), V)
            inc(V.tensor_scalar(out=auxs[:, 8:9], in0=hmax[:, tb:tb + 1],
                                scalar1=1.0 / 127.0, scalar2=None, op0=ALU.mult), V)
            vm = inc(V.reciprocal(out=auxs[:, 9:10], in_=auxs[:, 8:9]), V)
            wt(S, "mV", vm)
            for qt in range(4):
                sl = slice(qt * D, (qt + 1) * D)
                wt(S, QS[qt], dqd[qt])
                a = S.activation(out=q2_u[:], in_=hbuf[:, sl], func=AF.Identity,
                                 scale=auxs[:, 9:10], bias=F16B)
                sm = inc(a, S)
                act_q[qt] = sm
                s_last[tb] = sm
                wt(V, "mS", sm)
                inc(V.tensor_copy(out=q2_h[:], in_=q2_u[:]), V)
                vm = inc(V.tensor_scalar(out=hqtm[:], in0=q2_h[:], scalar1=-F16B,
                                         scalar2=None, op0=ALU.add), V)
                wt(T, "mV", vm)
                for i4 in range(4):
                    b = 4 + i4
                    wt(T, "mV", tb_ev[b])
                    for sl4 in range(4):
                        i2 = i4 * 4 + sl4
                        mm = T.matmul(out=bank(b)[:, sl4 * 128:(sl4 + 1) * 128],
                                      lhsT=hqtm[:, i2 * 128:(i2 + 1) * 128],
                                      rhs=identb[:], start=True, stop=True)
                    tm = inc(mm, T)
                    wt(V, "mT", tm)
                    ev = V.tensor_copy(
                        out=hqfm[:, qt * FD + i4 * 4:qt * FD + i4 * 4 + 4, tb * 128:(tb + 1) * 128],
                        in_=bank(b)[:].rearrange("p (f c) -> p f c", c=128))
                    tb_ev[b] = inc(ev, V)
                wt(S, "mV", vm)
        inc(V.tensor_tensor(out=dscl[:], in0=hmax[:], in1=xnmax[:], op=ALU.mult), V)
        inc(V.tensor_scalar(out=dscl[:], in0=dscl[:], scalar1=thr[:, 7:8],
                            scalar2=None, op0=ALU.mult), V)
        p2b_V = cnt["mV"]
        p2b_S = cnt["mS"]

        maybe_stop('P2b')
        dsrc = wd_d.rearrange("(hc p) (oc c) -> hc p oc c", p=128, c=128)
        wt(SY, "mS", p2b_S)
        wt(V, "mS", p2b_S)
        ev_bank[4] = p2b_V
        tmq_prev = [0, 0, 0, 0]
        e_done = {0: 0, 1: 0}
        pend_c = None
        for oc in range(FD):
            qd_q = [0] * 4
            for ei in range(8):
                slab = dslab[:, 0:8, :] if ei % 2 == 0 else dslab[:, 8:16, :]
                ssem = "w0" if ei % 2 == 0 else "w1"
                if e_done[ei % 2]:
                    wt(SY, "mV", e_done[ei % 2])
                dd = dinc(SY.dma_start(out=slab,
                                       in_=dsrc[ei * 8:(ei + 1) * 8, :, oc, :].rearrange("f p c -> p f c")), ssem)
                fl = slab.rearrange("p f c -> p (f c)")
                fa = dcma[:, 0:8, :].rearrange("p f c -> p (f c)")
                fq = tdq[:, ei * 8:(ei + 1) * 8, :].rearrange("p f c -> p (f c)")
                wt(V, ssem, dd)
                if ei % 2 == 0:
                    wt(V, "mT", tmq_prev[ei // 2])
                inc(V.tensor_scalar(out=fa, in0=fl, scalar1=thr[:, 5:6],
                                    scalar2=None, op0=ALU.is_lt), V)
                e_done[ei % 2] = inc(V.scalar_tensor_tensor(
                    out=fq, in0=fl, scalar=thr[:, 4:5], in1=fa,
                    op0=ALU.is_gt, op1=ALU.subtract), V)
                if ei % 2 == 1:
                    qd_q[ei // 2] = e_done[1]
            if pend_c is not None:
                eoc, ebp, etm = pend_c
                wt(V, "mT", etm)
                ev = V.tensor_copy(out=ffnb[:, eoc, :], in_=bank2(ebp))
                ev_bank[ebp] = inc(ev, V)
            bp = 2 if oc % 2 == 0 else 4
            for qt in range(4):
                wt(T, "mV", qd_q[qt])
                if qt == 0:
                    wt(T, "mV", ev_bank.get(bp, 0))
                for nh in range(2):
                    for hc16 in range(FD):
                        hcc = qt * FD + hc16
                        mm = T.matmul(out=bank(bp + nh), lhsT=tdq[:, hcc, :],
                                      rhs=hqfm[:, hcc, nh * 512:(nh + 1) * 512],
                                      start=(hcc == 0), stop=(hcc == 63))
                tmq_prev[qt] = inc(mm, T)
            pend_c = (oc, bp, tmq_prev[3])
        eoc, ebp, etm = pend_c
        wt(V, "mT", etm)
        ev = V.tensor_copy(out=ffnb[:, eoc, :], in_=bank2(ebp))
        ev_bank[ebp] = inc(ev, V)

        maybe_stop('P2c')
        wt(G, "mT", tmq_prev[0])
        wt(G, "g0", aos_dg[FD - 2])
        wt(G, "g1", aos_dg[FD - 1])
        ao_dd = dinc(G.dma_start(out=aofm[:], in_=aos_d.rearrange("f p c -> p f c")), "g2")
        wt(SY, "mT", tmq_prev[2])
        wt(T, "g2", ao_dd)
        wt(T, "mV", cnt["mV"])
        y_ev = [0] * NQT
        y_dg = [0] * NQT
        s_cp = [0] * NQT
        v_stt = [0] * NQT
        for tb in range(NQT):
            buf = xtb[:, tb % 2, :]
            if tb >= 2:
                wt(SY, "mV", y_ev[tb - 2])
            dd = dinc(SY.dma_start(out=buf, in_=xo_d[tb * 128:(tb + 1) * 128, :]),
                      "x0" if tb % 2 == 0 else "x1")
            if tb >= 1:
                wt(T, "mS", s_cp[tb - 1])
            for oc in range(FD):
                bt = (psb0, psb1, psb6, psb7)[oc // 4]
                tm_ = T.matmul(out=bt[:, (oc % 4) * 128:(oc % 4) * 128 + 128],
                               lhsT=aofm[:, oc, tb * 128:(tb + 1) * 128],
                               rhs=identb[:], start=True, stop=True)
            tm = inc(tm_, T)
            if tb >= 1:
                wt(T, "mV", v_stt[tb - 1])
            for oc in range(FD):
                pt = psp23 if oc < 8 else psp45
                tm2_ = T.matmul(out=pt[:, (oc % 8) * 128:(oc % 8) * 128 + 128],
                                lhsT=ffnb[:, oc, tb * 128:(tb + 1) * 128],
                                rhs=identb[:], start=True, stop=True)
            tm2 = inc(tm2_, T)
            ys = ystg[:, tb % 2, :]
            wt(S, "mT", tm)
            if tb >= 2:
                wt(S, "g0" if tb % 2 == 0 else "g1", y_dg[tb - 2])
            inc(S.activation(out=ys[:, 0:512], in_=psb0[:], func=AF.Copy), S)
            inc(S.activation(out=ys[:, 512:1024], in_=psb1[:], func=AF.Copy), S)
            inc(S.activation(out=ys[:, 1024:1536], in_=psb6[:], func=AF.Copy), S)
            sm4 = inc(S.activation(out=ys[:, 1536:2048], in_=psb7[:], func=AF.Copy), S)
            s_cp[tb] = sm4
            wt(V, "mS", sm4)
            wt(V, "mT", tm2)
            wt(V, "x0" if tb % 2 == 0 else "x1", dd)
            inc(V.scalar_tensor_tensor(out=ys[:, 0:1024], in0=psp23[:],
                                       scalar=dscl[:, tb:tb + 1], in1=ys[:, 0:1024],
                                       op0=ALU.mult, op1=ALU.add), V)
            v_stt[tb] = inc(V.scalar_tensor_tensor(out=ys[:, 1024:2048], in0=psp45[:],
                                                   scalar=dscl[:, tb:tb + 1], in1=ys[:, 1024:2048],
                                                   op0=ALU.mult, op1=ALU.add), V)
            ev = V.tensor_tensor(out=ys, in0=ys, in1=buf, op=ALU.add)
            y_ev[tb] = inc(ev, V)
            wt(G, "mV", y_ev[tb])
            gd = G.dma_start(out=y_d[tb * 128:(tb + 1) * 128, :], in_=ystg[:, tb % 2, :])
            y_dg[tb] = dinc(gd, "g0" if tb % 2 == 0 else "g1")
        wt(G, "g0", cnt["g0"])
        wt(G, "g1", cnt["g1"])
        wt(SY, "g0", cnt["g0"])
        wt(SY, "g1", cnt["g1"])
        return cnt

    with nc.Block() as block:
        counts = {}

        def _sect(name):
            cnt = {k: 0 for k in ("c0", "x0", "x1", "w0", "w1", "g0", "g1", "g2", "p0", "p1",
                                  "mT", "mV", "mS", "mG", "mY")}
            try:
                counts[name] = emit(nc, name, cnt)
            except Exception as ex:
                if type(ex).__name__ == "_EarlyStop":
                    counts[name] = cnt
                else:
                    raise

        @block.sync
        def _(e):
            _sect("sync")

        @block.vector
        def _(e):
            _sect("vector")

        @block.scalar
        def _(e):
            _sect("scalar")

        @block.tensor
        def _(e):
            _sect("tensor")

        @block.gpsimd
        def _(e):
            _sect("gpsimd")

    base = counts["sync"]
    for k2, v2 in counts.items():
        assert v2 == base, (k2, v2, base)
    for cm in sem_cms:
        cm.__exit__(None, None, None)
    return nc


_BF = ml_dtypes.bfloat16


def _consts():
    masks = np.zeros((512, 512), dtype=_BF)
    for p in range(4):
        kk = np.arange(128)[:, None]
        qq = np.arange(512)[None, :]
        masks[p * 128:(p + 1) * 128, :] = (qq >= kk + p * 128).astype(_BF)
    ones = np.ones((128, 128), dtype=_BF)
    ident = np.eye(128, dtype=np.float32).astype(_BF)
    return masks, ones, ident


def kernel(L0_attn_x, L0_attn_w_dq, L0_attn_w_uq, L0_attn_w_dkv, L0_attn_w_ukv,
           L0_attn_w_out, L0_ffn_w_gate, L0_ffn_w_up, L0_ffn_w_down):
    x = np.asarray(L0_attn_x, dtype=np.float32)
    wdqT = np.ascontiguousarray(np.asarray(L0_attn_w_dq, np.float32).T)
    wuqT = np.ascontiguousarray(np.asarray(L0_attn_w_uq, np.float32).T)
    wdkvT = np.ascontiguousarray(np.asarray(L0_attn_w_dkv, np.float32).T)
    wukvT = np.ascontiguousarray(np.asarray(L0_attn_w_ukv, np.float32).T)
    woutT = np.ascontiguousarray(np.asarray(L0_attn_w_out, np.float32).T)
    wgT = np.ascontiguousarray(np.asarray(L0_ffn_w_gate, np.float32).T)
    wuT = np.ascontiguousarray(np.asarray(L0_ffn_w_up, np.float32).T)
    wdT = np.ascontiguousarray(np.asarray(L0_ffn_w_down, np.float32).T)
    masks, ones, ident = _consts()

    in_maps = []
    for c in range(NCORES):
        KA = 512 * (c + 1)
        KBn = NK - KA
        xk = np.ascontiguousarray(np.concatenate([x[0, :KA], x[1, :KBn]], axis=0))
        xo = np.ascontiguousarray(np.concatenate([x[0, KA - 512:KA], x[1, KBn - 512:KBn]], axis=0))
        in_maps.append(dict(xk=xk, xo=xo, wdqT=wdqT, wuqT=wuqT, wdkvT=wdkvT,
                            wukvT=wukvT, woutT=woutT, wgT=wgT, wuT=wuT, wdT=wdT,
                            masks=masks, onesc=ones, identc=ident,
                            cidc=np.full((1, 4), c, np.int32)))

    nc = build_nc()
    res = run_bass_kernel_spmd(nc, in_maps, core_ids=list(range(NCORES)))

    out = np.empty((2, 4096, D), dtype=np.float32)
    for c in range(NCORES):
        y = np.asarray(res.results[c]["y"], dtype=np.float32)
        KA = 512 * (c + 1)
        KBn = NK - KA
        out[0, KA - 512:KA] = y[0:512]
        out[1, KBn - 512:KBn] = y[512:1024]
    return out


# version 19
# speedup vs baseline: 1.2291x; 1.0082x over previous
import math
import numpy as np
import ml_dtypes
import concourse.bass as bass
import concourse.mybir as mybir
from concourse.bass_utils import run_bass_kernel_spmd

F32 = mybir.dt.float32
BF16 = mybir.dt.bfloat16
F16 = mybir.dt.float16
I32 = mybir.dt.int32
AF = mybir.ActivationFunctionType
ALU = mybir.AluOpType
AX = mybir.AxisListType.X

NCORES = 8
D = 2048
FD = 16
QCD = 12
KVCD = 4
HN = 16
FF = 8192
NK = 4608
NKC = 36
NQ = 1024
NQT = 8
HPG = 2
NGRP = HN // HPG
SCALE = 1.0 / math.sqrt(128.0)
CB = 256.0
F16B = 1536.0
EPS = 1e-5
EXPB = -5.0
KB = 1024
SB0 = 16512
NW = float(D) * FF
DEBUG = False
STOP_AFTER = None


def build_nc():
    nc = bass.Bass("TRN2", target_bir_lowering=False)

    xk_d = nc.dram_tensor("xk", [NK, D], F32, kind="ExternalInput").ap()
    xo_d = nc.dram_tensor("xo", [NQ, D], F32, kind="ExternalInput").ap()
    wdq_d = nc.dram_tensor("wdqT", [D, 1536], F32, kind="ExternalInput").ap()
    wuq_d = nc.dram_tensor("wuqT", [1536, D], F32, kind="ExternalInput").ap()
    wdkv_d = nc.dram_tensor("wdkvT", [D, 512], F32, kind="ExternalInput").ap()
    wukv_d = nc.dram_tensor("wukvT", [512, 2 * D], F32, kind="ExternalInput").ap()
    wout_d = nc.dram_tensor("woutT", [D, D], F32, kind="ExternalInput").ap()
    wg_d = nc.dram_tensor("wgT", [D, FF], F32, kind="ExternalInput").ap()
    wu_d = nc.dram_tensor("wuT", [D, FF], F32, kind="ExternalInput").ap()
    wd_d = nc.dram_tensor("wdT", [FF, D], F32, kind="ExternalInput").ap()
    masks_d = nc.dram_tensor("masks", [512, 512], BF16, kind="ExternalInput").ap()
    ones_d = nc.dram_tensor("onesc", [128, 128], BF16, kind="ExternalInput").ap()
    ident_d = nc.dram_tensor("identc", [128, 128], BF16, kind="ExternalInput").ap()
    y_d = nc.dram_tensor("y", [NQ, D], F32, kind="ExternalOutput").ap()
    hs_d = nc.dram_tensor("hs", [NQ, FF], F32).ap()
    aos_d = nc.dram_tensor("aos", [FD, 128, NQ], BF16).ap()
    xqs_d = nc.dram_tensor("xqs", [FD, 128, NQ], BF16).ap()

    AT = nc.alloc_sbuf_tensor_at
    identb = AT("identb", [128, 128], BF16, offset=SB0 + 0)
    onesb = AT("onesb", [128, 128], BF16, offset=SB0 + 256)
    masksb = AT("masksb", [128, 4, 512], BF16, offset=SB0 + 512)
    off = SB0 + 4608
    def nx(shape, dt, name):
        nonlocal off
        t = AT(name, shape, dt, offset=off)
        sz = 1
        for s in shape[1:]:
            sz *= s
        off += sz * (4 if dt in (F32, I32) else 2)
        off = (off + 31) & ~31
        return t
    sums_c = nx([128, 48], F32, "sums_c")
    sumsq_c = nx([128, 48], F32, "sumsq_c")
    lnt0 = nx([128, 48], F32, "lnt0")
    lnt1 = nx([128, 48], F32, "lnt1")
    lnt2 = nx([128, 48], F32, "lnt2")
    scq = nx([128, 8], F32, "scq")
    xnmax = nx([128, 8], F32, "xnmax")
    acc3 = nx([128, 4], F32, "acc3")
    accT = nx([128, 4], F32, "accT")
    thr = nx([128, 16], F32, "thr")
    sgs = nx([128, 8], F32, "sgs")
    hmax = nx([128, 8], F32, "hmax")
    dscl = nx([128, 8], F32, "dscl")
    hmaxa = nx([128, FD, 8], F32, "hmaxa")
    auxs = nx([128, 16], F32, "auxs")
    cpool = nx([128, 4], F32, "cpool")
    onesf = nx([128, 128], F32, "onesf")
    identf = nx([128, 128], F32, "identf")
    cids0 = nx([1, 16], I32, "cids0")
    assert off <= SB0 + 8 * KB, off
    nc.const_aps.register(F32, EPS, cpool.ap()[:, 0:1])
    nc.const_aps.register(F32, CB, cpool.ap()[:, 1:2])
    nc.const_aps.register(F32, EXPB, cpool.ap()[:, 2:3])
    nc.const_aps.register(F32, F16B, cpool.ap()[:, 3:4])

    ckv = AT("ckv", [128, KVCD, NK], BF16, offset=SB0 + 8 * KB)
    wukv = AT("wukv", [128, KVCD, 2 * D], BF16, offset=SB0 + 44 * KB)
    qfm = AT("qfm", [128, HN, NQ], BF16, offset=SB0 + 76 * KB)
    otall = AT("otall", [128, HN, NQ], BF16, offset=SB0 + 108 * KB)
    wdkv = AT("wdkv", [128, FD, 512], BF16, offset=SB0 + 108 * KB)
    xntb = AT("xntb", [128, 2, D], BF16, offset=SB0 + 140 * KB)
    xnf = AT("xnf", [128, 2, FD, 512], BF16, offset=SB0 + 148 * KB)
    statda = AT("statda", [128, D], F32, offset=SB0 + 124 * KB)
    xt0 = AT("xt0", [128, D], F32, offset=SB0 + 188 * KB)
    xt1 = AT("xt1", [128, D], F32, offset=SB0 + 196 * KB)
    xqt = AT("xqt", [128, NQT, D], BF16, offset=SB0 + 76 * KB)
    xnofm = AT("xnofm", [128, FD, NQ], BF16, offset=SB0 + 108 * KB)
    xnof = AT("xnof", [128, 2, D], F32, offset=SB0 + 148 * KB)
    qu_u = AT("qu_u", [128, D], F32, offset=SB0 + 164 * KB)
    qu_h = AT("qu_h", [128, D], F16, offset=SB0 + 172 * KB)
    qu_g = AT("qu_g", [128, D], F32, offset=SB0 + 180 * KB)
    wst_f = AT("wst_f", [128, 2, D], F32, offset=SB0 + 148 * KB)
    wst_b = AT("wst_b", [128, 2, FD, 128], BF16, offset=SB0 + 164 * KB)
    qcfm = AT("qcfm", [128, QCD, NQ], BF16, offset=SB0 + 172 * KB)
    kg = AT("kg", [128, HPG, NK], BF16, offset=SB0 + 140 * KB)
    vg = AT("vg", [128, NKC, HPG * 128], BF16, offset=SB0 + 158 * KB)
    ptile = AT("ptile", [128, 2, HPG, 512], BF16, offset=SB0 + 176 * KB)
    recip_s = AT("recip_s", [128, 1024], F32, offset=SB0 + 180 * KB)
    rep_s = AT("rep_s", [128, 2, 512], F32, offset=SB0 + 184 * KB)
    aostg = AT("aostg", [128, 2, NQ], BF16, offset=SB0 + 148 * KB)
    wof = AT("wof", [128, 2, D], F32, offset=SB0 + 188 * KB)
    wob = AT("wob", [128, 2, FD, 128], BF16, offset=SB0 + 140 * KB)
    gsq = AT("gsq", [128, 2, 4, 512], F32, offset=SB0 + 8 * KB)
    usq = AT("usq", [128, 2, 4, 512], F32, offset=SB0 + 24 * KB)
    tgq2 = AT("tgq2", [128, FD, 512], BF16, offset=SB0 + 40 * KB)
    tuq2 = AT("tuq2", [128, FD, 512], BF16, offset=SB0 + 56 * KB)
    tgq = AT("tgq", [128, FD, 512], BF16, offset=SB0 + 72 * KB)
    tuq = AT("tuq", [128, FD, 512], BF16, offset=SB0 + 88 * KB)
    xqfm = AT("xqfm", [128, FD, NQ], BF16, offset=SB0 + 104 * KB)
    hg = AT("hg", [128, NQT, 512], F32, offset=SB0 + 136 * KB)
    hstg = AT("hstg", [128, NQT, 512], F32, offset=SB0 + 152 * KB)
    cmp_a = AT("cmp_a", [128, 4, 512], BF16, offset=SB0 + 200 * KB)
    hqfm = AT("hqfm", [128, 64, NQ], BF16, offset=SB0 + 40 * KB)
    hbuf = AT("hbuf", [128, FF], F32, offset=SB0 + 168 * KB)
    q2_h = AT("q2_h", [128, D], F16, offset=SB0 + 8 * KB)
    q2_u = AT("q2_u", [128, D], F32, offset=SB0 + 28 * KB)
    hqtm = AT("hqtm", [128, D], BF16, offset=SB0 + 24 * KB)
    ffnb = AT("ffnb", [128, FD, NQ], BF16, offset=SB0 + 8 * KB)
    dslab = AT("dslab", [128, FD, 128], F32, offset=SB0 + 168 * KB)
    dcma = AT("dcma", [128, FD, 128], BF16, offset=SB0 + 176 * KB)
    tdq = AT("tdq", [128, 64, 128], BF16, offset=SB0 + 184 * KB)
    pa0 = AT("pa0", [128, D], F32, offset=SB0 + 76 * KB)
    pa1 = AT("pa1", [128, D], F32, offset=SB0 + 84 * KB)
    wdst = AT("wdst", [128, FD, 256], F32, offset=SB0 + 92 * KB)
    pc0 = AT("pc0", [128, D], F32, offset=SB0 + 180 * KB)
    pd0 = AT("pd0", [128, D], F32, offset=SB0 + 196 * KB)
    pe0 = AT("pe0", [128, D], F32, offset=SB0 + 108 * KB)
    pe1 = AT("pe1", [128, D], F32, offset=SB0 + 116 * KB)
    pt0 = AT("pt0", [128, D], F32, offset=SB0 + 158 * KB)
    pt1 = AT("pt1", [128, D], F32, offset=SB0 + 166 * KB)
    accTw = AT("accTw", [128, 3, 64], F32, offset=SB0 + 204 * KB)
    aofm = AT("aofm", [128, FD, NQ], BF16, offset=SB0 + 40 * KB)
    sumb = AT("sumb", [128, FD, NQ], BF16, offset=SB0 + 72 * KB)
    xtb = AT("xtb", [128, 2, D], F32, offset=SB0 + 104 * KB)
    ystg = AT("ystg", [128, 2, D], F32, offset=SB0 + 120 * KB)

    psb0 = nc.alloc_psum_tensor("psb0", [128, 512], F32)
    psb1 = nc.alloc_psum_tensor("psb1", [128, 512], F32)
    psp23 = nc.alloc_psum_tensor("psp23", [128, 1024], F32)
    psp45 = nc.alloc_psum_tensor("psp45", [128, 1024], F32)
    psb6 = nc.alloc_psum_tensor("psb6", [128, 512], F32)
    psb7 = nc.alloc_psum_tensor("psb7", [128, 512], F32)

    def bank(i):
        if i == 0:
            return psb0[:]
        if i == 1:
            return psb1[:]
        if i in (2, 3):
            return psp23[:, (i - 2) * 512:(i - 2) * 512 + 512]
        if i in (4, 5):
            return psp45[:, (i - 4) * 512:(i - 4) * 512 + 512]
        if i == 6:
            return psb6[:]
        return psb7[:]

    def bank2(i):
        return psp23[:] if i == 2 else psp45[:]

    cid_d = nc.dram_tensor("cidc", [1, 4], I32, kind="ExternalInput").ap()
    cids = cids0

    sem_cms, sems = [], {}
    for s in ("c0", "x0", "x1", "w0", "w1", "g0", "g1", "g2", "p0", "p1",
              "mT", "mV", "mS", "mG", "mY"):
        cm = nc.semaphore(s)
        sems[s] = cm.__enter__()
        sem_cms.append(cm)

    class _Dummy:
        def then_inc(self, *a, **k):
            return self

        def _wait_ge(self, *a, **k):
            return self

    class _NullGuard:
        def __enter__(self):
            return self

        def __exit__(self, *a):
            return False

    _DUM = _Dummy()
    _NG = _NullGuard()

    class EP:
        def __init__(self, real, on):
            self._r = real
            self._on = on

        def __getattr__(self, name):
            if self._on:
                return getattr(self._r, name)
            if name in ("If_eq", "If_ne", "If_lt"):
                return lambda *a, **k: _NG
            return lambda *a, **k: _DUM

    DMA_SEMS = ("c0", "x0", "x1", "w0", "w1", "g0", "g1", "g2", "p0", "p1")

    def emit(nc, ACTIVE, cnt):
        real = {"tensor": nc.tensor, "vector": nc.vector, "scalar": nc.scalar,
                "gpsimd": nc.gpsimd, "sync": nc.sync}
        T = EP(nc.tensor, ACTIVE == "tensor")
        V = EP(nc.vector, ACTIVE == "vector")
        S = EP(nc.scalar, ACTIVE == "scalar")
        G = EP(nc.gpsimd, ACTIVE == "gpsimd")
        SY = EP(nc.sync, ACTIVE == "sync")
        e2s = {id(T): "mT", id(V): "mV", id(S): "mS", id(G): "mG", id(SY): "mY"}
        grd = {"tensor": T, "vector": V, "scalar": S}.get(ACTIVE)

        def inc(instr, eng):
            s = e2s[id(eng)]
            cnt[s] += 1
            instr.then_inc(sems[s], 1)
            eng.wait_ge(sems[s], cnt[s])
            return cnt[s]

        def dinc(instr, q):
            cnt[q] += 16
            instr.then_inc(sems[q], 16)
            return cnt[q]

        def wt(eng, s, val):
            if val > 0:
                eng.wait_ge(sems[s], val)

        ENGS = [("mT", T), ("mV", V), ("mS", S), ("mG", G), ("mY", SY)]

        def barrier():
            for nm, _e in ENGS:
                cnt[nm] += 1
            me = EP.__getattribute__(
                {"tensor": T, "vector": V, "scalar": S, "gpsimd": G, "sync": SY}[ACTIVE],
                "_r") if False else None
            ap = {"tensor": T, "vector": V, "scalar": S, "gpsimd": G, "sync": SY}[ACTIVE]
            ap.nop().then_inc(sems["m" + {"tensor": "T", "vector": "V", "scalar": "S",
                                          "gpsimd": "G", "sync": "Y"}[ACTIVE]], 1)
            for nm, _e in ENGS:
                if nm != "m" + {"tensor": "T", "vector": "V", "scalar": "S",
                                "gpsimd": "G", "sync": "Y"}[ACTIVE]:
                    ap.wait_ge(sems[nm], cnt[nm])
            for dq in DMA_SEMS:
                if cnt[dq]:
                    ap.wait_ge(sems[dq], cnt[dq])

        class _EarlyStop(Exception):
            pass

        def maybe_stop(tag):
            if STOP_AFTER == tag:
                barrier()
                gd = G.dma_start(out=y_d[0:128, 0:128], in_=identb[:])
                dinc(gd, "g0")
                wt(G, "g0", cnt["g0"])
                wt(SY, "g0", cnt["g0"])
                raise _EarlyStop()

        for p in range(4):
            dinc(SY.dma_start(out=masksb[:, p, :], in_=masks_d[p * 128:(p + 1) * 128, :]), "c0")
        dinc(SY.dma_start(out=onesb[:], in_=ones_d), "c0")
        dinc(SY.dma_start(out=identb[:], in_=ident_d), "c0")
        cid_dd = dinc(SY.dma_start(out=cids[0:1, 0:4], in_=cid_d), "c0")
        wt(V, "c0", cnt["c0"])
        inc(V.memset(acc3[:], 0.0), V)
        inc(V.memset(cpool[:, 0:1], EPS), V)
        inc(V.memset(cpool[:, 1:2], CB), V)
        inc(V.memset(cpool[:, 2:3], EXPB), V)
        inc(V.memset(cpool[:, 3:4], F16B), V)
        inc(V.memset(onesf[:], 1.0), V)
        vcp = inc(V.tensor_copy(out=identf[:], in_=identb[:]), V)
        wt(S, "mV", vcp)

        maybe_stop("CONST")
        pre_dd = [dinc(SY.dma_start(out=xt0[:], in_=xk_d[0:128, :]), "x0"),
                  dinc(SY.dma_start(out=xt1[:], in_=xk_d[128:256, :]), "x1")]
        wdkv_dd = dinc(G.dma_start(out=wdkv[:],
                                   in_=wdkv_d.rearrange("(f p) c -> p f c", p=128)), "g2")
        maybe_stop("WDKV")

        def ln_stats(buf, col, dmy=None):
            dmy = statda if dmy is None else dmy
            a = S.activation(out=dmy[:], in_=buf[:], func=AF.Square,
                             accum_out=sumsq_c[:, col:col + 1])
            inc(a, S)
            a = S.activation(out=dmy[:], in_=buf[:], func=AF.Identity,
                             accum_out=sums_c[:, col:col + 1])
            return inc(a, S)

        def ln_norm(buf, col, st_sm, own_idx=None):
            wt(V, "mS", st_sm)
            inc(V.tensor_scalar(out=lnt0[:, col:col + 1], in0=sums_c[:, col:col + 1],
                                scalar1=1.0 / D, scalar2=None, op0=ALU.mult), V)
            inc(V.scalar_tensor_tensor(out=lnt1[:, col:col + 1], in0=lnt0[:, col:col + 1],
                                       scalar=1.0, in1=lnt0[:, col:col + 1],
                                       op0=ALU.mult, op1=ALU.mult), V)
            vm = inc(V.scalar_tensor_tensor(out=lnt1[:, col:col + 1], in0=sumsq_c[:, col:col + 1],
                                            scalar=1.0 / D, in1=lnt1[:, col:col + 1],
                                            op0=ALU.mult, op1=ALU.subtract), V)
            wt(S, "mV", vm)
            inc(S.activation(out=lnt1[:, col:col + 1], in_=lnt1[:, col:col + 1],
                             func=AF.Ln, bias=EPS), S)
            sm = inc(S.activation(out=lnt1[:, col:col + 1], in_=lnt1[:, col:col + 1],
                                  func=AF.Exp, scale=-0.5), S)
            wt(V, "mS", sm)
            vm = inc(V.scalar_tensor_tensor(out=lnt2[:, col:col + 1], in0=lnt0[:, col:col + 1],
                                            scalar=-1.0, in1=lnt1[:, col:col + 1],
                                            op0=ALU.mult, op1=ALU.mult), V)
            wt(S, "mV", vm)
            sm = inc(S.activation(out=xntb[:, col % 2, :], in_=buf[:], func=AF.Identity,
                                  scale=lnt1[:, col:col + 1], bias=lnt2[:, col:col + 1]), S)
            if own_idx is not None:
                sm = inc(S.activation(out=xnof[:, own_idx % 2, :], in_=buf[:], func=AF.Identity,
                                      scale=lnt1[:, col:col + 1], bias=lnt2[:, col:col + 1]), S)
            return sm

        tb_ev = {4: 0, 5: 0, 6: 0, 7: 0}

        def transpose_tile(src_tile, dst_fn, wait_s=0, wait_v=0):
            if wait_s:
                wt(T, "mS", wait_s)
            if wait_v:
                wt(T, "mV", wait_v)
            for fc in range(FD):
                b = 4 + (fc % 4)
                wt(T, "mV", tb_ev[b])
                mm = T.matmul(out=bank(b)[:, 0:128],
                              lhsT=src_tile[:, fc * 128:(fc + 1) * 128],
                              rhs=identb[:], start=True, stop=True)
                tm = inc(mm, T)
                wt(V, "mT", tm)
                ev = V.tensor_copy(out=dst_fn(fc), in_=bank(b)[:, 0:128])
                tb_ev[b] = inc(ev, V)

        scale_blocks = []
        for mi, wsrc in enumerate((wg_d, wu_d, wd_d)):
            for rb in range(wsrc.shape[0] // 128):
                for sb in range(wsrc.shape[1] // D):
                    scale_blocks.append((wsrc, rb, sb, mi))
        sc_i = {"v": 0}
        sc_done = {}

        def pump_scale(n, bufs=None, on_s=False, dq=None, psem="p0"):
            if bufs is None:
                bufs = (xt0, xt1)
            if dq is None:
                dq = G
            for _ in range(n):
                k = sc_i["v"]
                if k >= len(scale_blocks):
                    return
                sc_i["v"] += 1
                wsrc, rb, sb, mi = scale_blocks[k]
                slot = k % len(bufs)
                sbuf = bufs[slot]
                ps = psem if len(bufs) == 1 else ("p0" if slot == 0 else "p1")
                col = k - 64 * mi
                prev = sc_done.get(id(sbuf))
                if prev:
                    wt(dq, prev[0], prev[1])
                lastp = sc_done.get(ps)
                if lastp and lastp != prev:
                    wt(dq, lastp[0], lastp[1])
                dd = dinc(dq.dma_start(out=sbuf[:],
                                       in_=wsrc[rb * 128:(rb + 1) * 128, sb * D:(sb + 1) * D]),
                          ps)
                if on_s:
                    wt(S, ps, dd)
                    sm = inc(S.activation(out=sbuf[:], in_=sbuf[:], func=AF.Abs,
                                          accum_out=accTw[:, mi, col:col + 1]), S)
                    sc_done[id(sbuf)] = sc_done[ps] = ("mS", sm)
                else:
                    wt(V, ps, dd)
                    vm = inc(V.reduce_sum(out=accTw[:, mi, col:col + 1], in_=sbuf[:], axis=AX,
                                          apply_absolute_value=True), V)
                    sc_done[id(sbuf)] = sc_done[ps] = ("mV", vm)

        xn_sm = [0] * (NKC + NQT)
        tp_done = [0] * (NKC + NQT)
        dkv_done = [0] * 12
        st_sm = [0] * (NKC + NQT)
        dds = [0] * NKC
        dds[0], dds[1] = pre_dd
        wt(S, "x0", dds[0])
        st_sm[0] = ln_stats(xt0, 0)
        for it in range(NKC):
            buf = xt0 if it % 2 == 0 else xt1
            if it + 1 < NKC:
                nbuf = xt1 if it % 2 == 0 else xt0
                nsem = "x1" if it % 2 == 0 else "x0"
                if it + 1 >= 2:
                    wt(SY, "mS", xn_sm[it - 1])
                    dds[it + 1] = dinc(SY.dma_start(out=nbuf[:],
                                                    in_=xk_d[(it + 1) * 128:(it + 2) * 128, :]), nsem)
                wt(S, nsem, dds[it + 1])
                st_sm[it + 1] = ln_stats(nbuf, it + 1)
            if it >= 2:
                wt(S, "mV", tp_done[it - 2])
            xn_sm[it] = ln_norm(buf, it, st_sm[it])
            gi, sl = it // 4, it % 4

            def dstf(fc, gi=gi, sl=sl):
                return xnf[:, gi % 2, fc, sl * 128:(sl + 1) * 128]

            wv = dkv_done[gi - 2] if gi >= 2 and sl == 0 else 0
            transpose_tile(xntb[:, it % 2, :], dstf, wait_s=xn_sm[it], wait_v=wv)
            tp_done[it] = cnt["mV"]
            if sl == 3:
                wt(T, "mV", tp_done[it])
                if gi == 0:
                    wt(T, "g2", wdkv_dd)
                for cc in range(KVCD):
                    for fc in range(FD):
                        mm = T.matmul(out=bank(cc), lhsT=wdkv[:, fc, cc * 128:(cc + 1) * 128],
                                      rhs=xnf[:, gi % 2, fc, :],
                                      start=(fc == 0), stop=(fc == FD - 1))
                    tm = inc(mm, T)
                    wt(V, "mT", tm)
                    ev = V.tensor_copy(out=ckv[:, cc, gi * 512:(gi + 1) * 512], in_=bank(cc))
                    dkv_done[gi] = inc(ev, V)
            pump_scale(1, (pa0, pa1))
        barrier()

        maybe_stop('P0a')
        xq_vm = [0] * NQT
        bdd = [0] * NQT
        bdd[0] = dinc(SY.dma_start(out=xt0[:], in_=xo_d[0:128, :]), "x0")
        wt(S, "x0", bdd[0])
        st_sm[NKC] = ln_stats(xt0, NKC, dmy=qu_g)
        for it in range(NQT):
            buf = xt0 if it % 2 == 0 else xt1
            if it + 1 < NQT:
                nbuf = xt1 if it % 2 == 0 else xt0
                nsem = "x1" if it % 2 == 0 else "x0"
                if it + 1 >= 2:
                    wt(SY, "mS", xn_sm[NKC + it - 1])
                bdd[it + 1] = dinc(SY.dma_start(out=nbuf[:],
                                                in_=xo_d[(it + 1) * 128:(it + 2) * 128, :]), nsem)
                wt(S, nsem, bdd[it + 1])
                st_sm[NKC + it + 1] = ln_stats(nbuf, NKC + it + 1, dmy=qu_g)
            if it >= 2:
                wt(S, "mV", tp_done[NKC + it - 2])
            if it >= 1:
                wt(S, "mV", xq_vm[it - 1])
            xn_sm[NKC + it] = ln_norm(buf, NKC + it, st_sm[NKC + it], own_idx=it)
            sm = xn_sm[NKC + it]
            wt(V, "mS", sm)
            inc(V.reduce_max(out=xnmax[:, it:it + 1], in_=xnof[:, it % 2, :], axis=AX,
                             apply_absolute_value=True), V)
            inc(V.tensor_scalar(out=scq[:, it:it + 1], in0=xnmax[:, it:it + 1],
                                scalar1=1.0 / 127.0, scalar2=1e-5 / 127.0,
                                op0=ALU.mult, op1=ALU.max), V)
            vm = inc(V.reciprocal(out=scq[:, it:it + 1], in_=scq[:, it:it + 1]), V)
            wt(S, "mV", vm)
            a = S.activation(out=qu_u[:], in_=xnof[:, it % 2, :], func=AF.Identity,
                             scale=scq[:, it:it + 1], bias=F16B)
            sm2 = inc(a, S)
            wt(V, "mS", sm2)
            inc(V.tensor_copy(out=qu_h[:], in_=qu_u[:]), V)
            xq_vm[it] = inc(V.tensor_scalar(out=xqt[:, it, :], in0=qu_h[:], scalar1=-F16B,
                                            scalar2=None, op0=ALU.add), V)

            def dsto(fc, it=it):
                return xnofm[:, fc, it * 128:(it + 1) * 128]

            transpose_tile(xntb[:, (NKC + it) % 2, :], dsto, wait_s=sm)
            tp_done[NKC + it] = cnt["mV"]
        barrier()

        maybe_stop('P0b')
        wukv_dd = 0
        for cc in range(KVCD):
            wukv_dd = dinc(G.dma_start(out=wukv[:, cc, :],
                                       in_=wukv_d[cc * 128:(cc + 1) * 128, :]), "g2")
        xq_dg = [0] * NQT
        for it in range(NQT):
            def dstq(fc, it=it):
                return xnf[:, it % 2, fc, 0:128]

            wv = xq_dg[it - 2] if it >= 2 else 0
            if wv:
                wt(V, "g0" if it % 2 == 0 else "g1", wv)
            transpose_tile(xqt[:, it, :], dstq, wait_v=xq_vm[it])
            vm = cnt["mV"]
            wt(G, "mV", vm)
            gd = G.dma_start(out=xqs_d[:, :, it * 128:(it + 1) * 128].rearrange("f p c -> p f c"),
                             in_=xnf[:, it % 2, :, 0:128])
            xq_dg[it] = dinc(gd, "g0" if it % 2 == 0 else "g1")
            pump_scale(1, (pc0,), on_s=True)
        barrier()

        maybe_stop('P0c')
        wcast_done = {0: 0, 1: 0}

        def stream_blocks(src, nchunks, col0, half, pre=None):
            if pre is None:
                if wcast_done[half]:
                    wt(SY, "mV", wcast_done[half])
                dd = dinc(SY.dma_start(
                    out=wst_f[:, half, 0:nchunks * 128].rearrange("p (f c) -> p f c", c=128),
                    in_=src[0:nchunks * 128, col0:col0 + 128].rearrange("(f p) c -> p f c", p=128)),
                    "w0" if half == 0 else "w1")
            else:
                dd = pre
            wt(V, "w0" if half == 0 else "w1", dd)
            wcast_done[half] = inc(
                V.tensor_copy(out=wst_b[:, half, 0:nchunks, :].rearrange("p f c -> p (f c)"),
                              in_=wst_f[:, half, 0:nchunks * 128]), V)
            return wcast_done[half]

        ev_bank = {}
        wv_next = stream_blocks(wdq_d, FD, 0, 0)
        for qc in range(QCD):
            wv = wv_next
            if qc + 1 < QCD:
                wv_next = stream_blocks(wdq_d, FD, (qc + 1) * 128, (qc + 1) % 2)
            wt(T, "mV", wv)
            for half in range(2):
                b = (qc * 2 + half) % 4
                wt(T, "mV", ev_bank.get(b, 0))
                for fc in range(FD):
                    mm = T.matmul(out=bank(b), lhsT=wst_b[:, qc % 2, fc, :],
                                  rhs=xnofm[:, fc, half * 512:(half + 1) * 512],
                                  start=(fc == 0), stop=(fc == FD - 1))
                tm = inc(mm, T)
                wt(V, "mT", tm)
                ev = V.tensor_copy(out=qcfm[:, qc, half * 512:(half + 1) * 512], in_=bank(b))
                ev_bank[b] = inc(ev, V)
            pump_scale(1, (pd0,), on_s=True)
            if qc == QCD - 2:
                wt(SY, "mV", wcast_done[0])
                uq0_dd = dinc(SY.dma_start(
                    out=wst_f[:, 0, 0:QCD * 128].rearrange("p (f c) -> p f c", c=128),
                    in_=wuq_d[0:QCD * 128, 0:128].rearrange("(f p) c -> p f c", p=128)), "w0")
        wt(G, "mT", cnt["mT"])
        wv_next = stream_blocks(wuq_d, QCD, 0, 0, pre=uq0_dd)
        for oc in range(FD):
            wv = wv_next
            if oc + 1 < FD:
                wv_next = stream_blocks(wuq_d, QCD, (oc + 1) * 128, (oc + 1) % 2)
            wt(T, "mV", wv)
            for half in range(2):
                b = (oc * 2 + half) % 4
                wt(T, "mV", ev_bank.get(b, 0))
                for qc in range(QCD):
                    mm = T.matmul(out=bank(b), lhsT=wst_b[:, oc % 2, qc, :],
                                  rhs=qcfm[:, qc, half * 512:(half + 1) * 512],
                                  start=(qc == 0), stop=(qc == QCD - 1))
                tm = inc(mm, T)
                wt(V, "mT", tm)
                ev = V.tensor_copy(out=qfm[:, oc, half * 512:(half + 1) * 512], in_=bank(b))
                ev_bank[b] = inc(ev, V)
            pump_scale(2, (pe0, pe1), on_s=True)
        p0e_T = cnt["mT"]
        p0e_V = cnt["mV"]
        p0e_S = cnt["mS"]

        maybe_stop('P0de')
        cid_reg = grd.alloc_register("cidreg") if grd else None
        if grd:
            wt(grd, "c0", cid_dd)
            grd.reg_load(cid_reg, cids[0:1, 0:1])

        def emit_attn_sweep(cid, h0, kv_ready, fin_in=0, gate_s=0):
            nA = 4 * (cid + 1)
            half_first = {0: 0, 1: nA}
            half_last = {0: nA - 1, 1: NKC - 1}
            t0, v0, s0 = cnt["mT"], cnt["mV"], cnt["mS"]
            jlo = 0
            wt(T, "mV", kv_ready)
            wt(T, "mV", fin_in)
            wt(V, "mS", gate_s)
            exp_sm = {}
            post_tm = {}

            def boundary(half):
                wt(V, "mT", cnt["mT"])
                inc(V.reciprocal(out=recip_s[0:64, 0:512], in_=psb6[0:64, :]), V)
                vm1 = inc(V.reciprocal(out=recip_s[0:64, 512:1024], in_=psb7[0:64, :]), V)
                fin = 0
                for hh in range(HPG):
                    wt(T, "mV", vm1 if hh == 0 else fin)
                    mm = T.matmul(out=bank(7), lhsT=onesf[0:1, :],
                                  rhs=recip_s[0:1, 512 * hh:512 * hh + 512],
                                  start=True, stop=True)
                    tmr = inc(mm, T)
                    wt(V, "mT", tmr)
                    inc(V.tensor_copy(out=rep_s[:, hh, :], in_=bank(7)), V)
                    ev = V.tensor_tensor(out=otall[:, h0 + hh, half * 512:(half + 1) * 512],
                                         in0=bank(hh), in1=rep_s[:, hh, :], op=ALU.mult)
                    fin = inc(ev, V)
                return fin

            fin_vm = 0
            nb = 0
            nmask = 0

            def emit_scores(j):
                half = 0 if j < nA else 1
                qc0 = half * 512
                sp = 2 + 2 * ((j - jlo) % 2)
                if (j - jlo) >= 2:
                    wt(T, "mS", exp_sm[j - 2])
                for hh in range(HPG):
                    mm = T.matmul(out=bank(sp + hh), lhsT=kg[:, hh, (j - jlo) * 128:(j - jlo + 1) * 128],
                                  rhs=qfm[:, h0 + hh, qc0:qc0 + 512], start=True, stop=True)
                tm = inc(mm, T)
                wt(S, "mT", tm)
                if (j - jlo) >= 2:
                    wt(S, "mT", post_tm[j - 2])
                pb = (j - jlo) % 2
                a = S.activation(out=ptile[:, pb, :, :].rearrange("p h c -> p (h c)"),
                                 in_=bank2(sp), func=AF.Exp, scale=SCALE, bias=EXPB)
                exp_sm[j] = inc(a, S)

            emit_scores(0)
            for j in range(NKC):
                half = 0 if j < nA else 1
                pb = (j - jlo) % 2
                if j + 1 < NKC and j + 1 != nA:
                    emit_scores(j + 1)
                if half == 0:
                    mp = j - (nA - 4)
                else:
                    mp = j - 32
                use_mask = 0 <= mp < 4
                if use_mask:
                    nmask += 1
                    wt(V, "mS", exp_sm[j])
                    inc(V.tensor_tensor(out=ptile[:, pb, 0, :], in0=ptile[:, pb, 0, :],
                                        in1=masksb[:, mp, :], op=ALU.mult), V)
                    pvm = inc(V.tensor_tensor(out=ptile[:, pb, 1, :], in0=ptile[:, pb, 1, :],
                                              in1=masksb[:, mp, :], op=ALU.mult), V)
                    wt(T, "mV", pvm)
                else:
                    wt(T, "mS", exp_sm[j])
                if fin_vm and j == nA:
                    wt(T, "mV", fin_vm)
                st = (j == half_first[half])
                sp_ = (j == half_last[half])
                for hh in range(HPG):
                    T.matmul(out=(psb6 if hh == 0 else psb7)[0:64, :],
                             lhsT=onesb[:, 0:64], rhs=ptile[:, pb, hh, :],
                             start=st, stop=sp_)
                for hh in range(HPG):
                    mm = T.matmul(out=bank(hh), lhsT=vg[:, j - jlo, hh * 128:(hh + 1) * 128],
                                  rhs=ptile[:, pb, hh, :], start=st, stop=sp_)
                post_tm[j] = inc(mm, T)
                if sp_:
                    fin_vm = boundary(half)
                    nb += 1
                    if j + 1 < NKC and j + 1 == nA:
                        emit_scores(j + 1)
            maxT = 2 * NKC + 4
            maxV = 16 + 12
            maxS = NKC
            dT = (t0 + maxT) - cnt["mT"]
            dV = (v0 + maxV) - cnt["mV"]
            dS = (s0 + maxS) - cnt["mS"]
            assert dT >= 0 and dV >= 0 and dS >= 0, (cid, dT, dV, dS)
            for _ in range(dT):
                inc(T.nop(), T)
            for _ in range(dV):
                inc(V.nop(), V)
            for _ in range(dS):
                inc(S.nop(), S)
            return fin_vm

        fin_prev = {c: 0 for c in range(NCORES)}
        ms_sweeps = 0
        for g in range(NGRP):
            h0 = g * HPG
            if ms_sweeps:
                wt(T, "mS", ms_sweeps)
            if g == 0:
                kv_ev = {2: ev_bank.get(2, 0), 3: ev_bank.get(3, 0),
                         4: tb_ev[4], 5: tb_ev[5]}
                wt(T, "mV", p0e_V)
                wt(T, "g2", wukv_dd)
            else:
                kv_ev = {}
            evi = 0
            for kt in range(9):
                k0 = kt * 512
                for hh in range(HPG):
                    b = 2 + (2 * kt + hh) % 4
                    wt(T, "mV", kv_ev.get(b, 0))
                    for cc in range(KVCD):
                        mm = T.matmul(out=bank(b),
                                      lhsT=wukv[:, cc, (h0 + hh) * 128:(h0 + hh + 1) * 128],
                                      rhs=ckv[:, cc, k0:k0 + 512],
                                      start=(cc == 0), stop=(cc == KVCD - 1))
                    tm = inc(mm, T)
                    wt(V, "mT", tm)
                    ev = V.tensor_copy(out=kg[:, hh, k0:k0 + 512], in_=bank(b))
                    kv_ev[b] = inc(ev, V)
                    evi += 1
                    if g > 0 and evi % 4 == 0:
                        pump_scale(1)
                    if g > 0 and evi in (2, 6):
                        pump_scale(1)
            for jj in range(NKC):
                b = 2 + (jj % 4)
                wt(T, "mV", kv_ev.get(b, 0))
                for cc in range(KVCD):
                    mm = T.matmul(out=bank(b)[:, 0:HPG * 128],
                                  lhsT=ckv[:, cc, jj * 128:(jj + 1) * 128],
                                  rhs=wukv[:, cc, 2048 + h0 * 128:2048 + (h0 + HPG) * 128],
                                  start=(cc == 0), stop=(cc == KVCD - 1))
                tm = inc(mm, T)
                wt(V, "mT", tm)
                ev = V.tensor_copy(out=vg[:, jj, :], in_=bank(b)[:, 0:HPG * 128])
                kv_ev[b] = inc(ev, V)
                evi += 1
                if g > 0 and evi % 4 == 0 and evi <= 52:
                    pump_scale(1)
            kv_ready = cnt["mV"]

            base_cnt = dict(cnt)
            finals = []
            fin_out = {}
            for cid in range(NCORES):
                for kk in cnt:
                    cnt[kk] = base_cnt[kk]
                with (grd.If_eq(cid_reg, cid) if grd else _NG):
                    fin_out[cid] = emit_attn_sweep(cid, h0, kv_ready, fin_prev[cid],
                                                   gate_s=p0e_S if g == 0 else 0)
                finals.append(dict(cnt))
            for f in finals:
                assert f == finals[0], (finals[0], f)
            for kk in cnt:
                cnt[kk] = finals[0][kk]
            fin_prev = fin_out
            ms_sweeps = cnt["mS"]
            if g == 0:
                wt(G, "mS", p0e_S)
                wt(G, "mV", p0e_V)
            maybe_stop("G%d" % g)
        barrier()

        for b_ in (xt0, xt1):
            pv = sc_done.get(id(b_))
            if pv:
                wt(SY, pv[0], pv[1])
        aos_dg = [0] * FD
        wcast_done = {0: 0, 1: 0}

        def stream_wout(oc):
            half = oc % 2
            if wcast_done[half]:
                wt(SY, "mV", wcast_done[half])
            dd = dinc(SY.dma_start(
                out=wof[:, half, :].rearrange("p (f c) -> p f c", c=128),
                in_=wout_d[:, oc * 128:(oc + 1) * 128].rearrange("(f p) c -> p f c", p=128)),
                "w0" if half == 0 else "w1")
            wt(V, "w0" if half == 0 else "w1", dd)
            wcast_done[half] = inc(
                V.tensor_copy(out=wob[:, half, :, :].rearrange("p f c -> p (f c)"),
                              in_=wof[:, half, :]), V)
            return wcast_done[half]

        wv_next = stream_wout(0)
        for oc in range(FD):
            wv = wv_next
            if oc + 1 < FD:
                wv_next = stream_wout(oc + 1)
            wt(T, "mV", wv)
            for half in range(2):
                b = (oc * 2 + half) % 4
                wt(T, "mV", ev_bank.get(b, 0))
                for ic in range(FD):
                    mm = T.matmul(out=bank(b), lhsT=wob[:, oc % 2, ic, :],
                                  rhs=otall[:, ic, half * 512:(half + 1) * 512],
                                  start=(ic == 0), stop=(ic == FD - 1))
                tm = inc(mm, T)
                wt(V, "mT", tm)
                if oc >= 2 and half == 0:
                    wt(V, "g0" if oc % 2 == 0 else "g1", aos_dg[oc - 2])
                ev = V.tensor_copy(out=aostg[:, oc % 2, half * 512:(half + 1) * 512], in_=bank(b))
                ev_bank[b] = inc(ev, V)
            wt(G, "mV", cnt["mV"])
            gd = G.dma_start(out=aos_d[oc], in_=aostg[:, oc % 2, :])
            aos_dg[oc] = dinc(gd, "g0" if oc % 2 == 0 else "g1")
        wt(SY, "mT", cnt["mT"])
        wt(SY, "mV", wcast_done[0])
        wt(SY, "mV", wcast_done[1])
        gsrc = wg_d.rearrange("(fc p) (hc c) -> fc p hc c", p=128, c=512)
        usrc = wu_d.rearrange("(fc p) (hc c) -> fc p hc c", p=128, c=512)
        p2a_xq = dinc(SY.dma_start(out=xqfm[:], in_=xqs_d.rearrange("f p c -> p f c")), "c0")
        p2a_g0 = dinc(SY.dma_start(out=gsq[:, 0, :, :],
                                   in_=gsrc[0:4, :, 0, :].rearrange("f p c -> p f c")), "w0")
        p2a_u0 = dinc(SY.dma_start(out=usq[:, 0, :, :],
                                   in_=usrc[0:4, :, 0, :].rearrange("f p c -> p f c")), "x0")
        wt(G, "mT", cnt["mT"])
        pump_scale(10 ** 9, bufs=(pt0, pt1), on_s=True)

        maybe_stop('P1out')
        wt(V, "mS", cnt["mS"])
        for mi in range(3):
            inc(V.reduce_sum(out=acc3[:, mi:mi + 1], in_=accTw[:, mi, :], axis=AX), V)
        vm = inc(V.tensor_copy(out=auxs[:, 0:4], in_=acc3[:, 0:4]), V)
        wt(T, "mV", vm)
        tm = inc(T.matmul(out=bank(0)[0:1, 0:4], lhsT=onesf[:, 0:1], rhs=auxs[:, 0:4],
                          start=True, stop=True), T)
        wt(V, "mT", tm)
        vm = inc(V.tensor_copy(out=auxs[0:1, 4:8], in_=bank(0)[0:1, 0:4]), V)
        wt(T, "mV", vm)
        tm = inc(T.matmul(out=bank(1)[:, 0:4], lhsT=onesf[0:1, :], rhs=auxs[0:1, 4:8],
                          start=True, stop=True), T)
        wt(V, "mT", tm)
        inc(V.tensor_copy(out=auxs[:, 8:12], in_=bank(1)[:, 0:4]), V)
        for (col, src, mul) in ((0, 0, 0.5 / NW), (1, 0, -0.5 / NW), (2, 1, 0.5 / NW),
                                (3, 1, -0.5 / NW), (4, 2, 0.5 / NW), (5, 2, -0.5 / NW),
                                (6, 0, 1.0 / (NW * 127.0))):
            inc(V.tensor_scalar(out=thr[:, col:col + 1], in0=auxs[:, 8 + src:9 + src],
                                scalar1=mul, scalar2=None, op0=ALU.mult), V)
        inc(V.scalar_tensor_tensor(out=thr[:, 7:8], in0=auxs[:, 10:11],
                                   scalar=1.0 / (NW * NW * 127.0 * 127.0),
                                   in1=auxs[:, 9:10], op0=ALU.mult, op1=ALU.mult), V)
        inc(V.tensor_scalar(out=sgs[:], in0=xnmax[:], scalar1=thr[:, 6:7],
                            scalar2=None, op0=ALU.mult), V)

        maybe_stop('SCL')
        wt(T, "c0", p2a_xq)
        wt(S, "g0", aos_dg[FD - 2])
        wt(S, "g1", aos_dg[FD - 1])

        hs_dg = [0] * FD
        tq_used = {0: 0, 1: 0}
        sq_done = {}
        silu_sm = [0] * NQT
        ev_tb = [0] * NQT
        pend = None
        for hc in range(FD):
            setq = hc % 2
            tg = tgq if setq == 0 else tgq2
            tu = tuq if setq == 0 else tuq2
            qg_q = [0] * 4
            qu_q = [0] * 4
            for q in range(4):
                gi = (hc * 4 + q) % 2
                gsem = "w0" if gi == 0 else "w1"
                usem = "x0" if gi == 0 else "x1"
                if hc == 0 and q == 0:
                    ddg, ddu = p2a_g0, p2a_u0
                else:
                    if sq_done.get(("g", gi)):
                        wt(SY, "mV", sq_done[("g", gi)])
                    ddg = dinc(SY.dma_start(out=gsq[:, gi, :, :],
                                            in_=gsrc[4 * q:4 * q + 4, :, hc, :].rearrange("f p c -> p f c")), gsem)
                    if sq_done.get(("u", gi)):
                        wt(SY, "mV", sq_done[("u", gi)])
                    ddu = dinc(SY.dma_start(out=usq[:, gi, :, :],
                                            in_=usrc[4 * q:4 * q + 4, :, hc, :].rearrange("f p c -> p f c")), usem)
                wt(V, gsem, ddg)
                if q == 0:
                    wt(V, "mT", tq_used[setq])
                fl = gsq[:, gi, :, :].rearrange("p f c -> p (f c)")
                fa = cmp_a[:, 0:4, :].rearrange("p f c -> p (f c)")
                inc(V.tensor_scalar(out=fa, in0=fl, scalar1=thr[:, 1:2],
                                    scalar2=None, op0=ALU.is_lt), V)
                fqg = tg[:, 4 * q:4 * q + 4, :].rearrange("p f c -> p (f c)")
                qg_q[q] = sq_done[("g", gi)] = inc(V.scalar_tensor_tensor(
                    out=fqg, in0=fl, scalar=thr[:, 0:1], in1=fa,
                    op0=ALU.is_gt, op1=ALU.subtract), V)
                wt(V, usem, ddu)
                flu = usq[:, gi, :, :].rearrange("p f c -> p (f c)")
                inc(V.tensor_scalar(out=fa, in0=flu, scalar1=thr[:, 3:4],
                                    scalar2=None, op0=ALU.is_lt), V)
                fqu = tu[:, 4 * q:4 * q + 4, :].rearrange("p f c -> p (f c)")
                qu_q[q] = sq_done[("u", gi)] = inc(V.scalar_tensor_tensor(
                    out=fqu, in0=flu, scalar=thr[:, 2:3], in1=fa,
                    op0=ALU.is_gt, op1=ALU.subtract), V)
            if pend is not None:
                ehc, etmu = pend
                for tb in range(NQT):
                    wt(V, "mT", etmu[tb])
                    if ehc >= 1 and tb == 0:
                        wt(V, "g2", hs_dg[ehc - 1])
                    inc(V.tensor_tensor(out=hstg[:, tb, :], in0=bank(tb), in1=hg[:, tb, :],
                                        op=ALU.mult), V)
                    ev_tb[tb] = cnt["mV"]
                    inc(V.reduce_max(out=hmaxa[:, ehc, tb:tb + 1], in_=hstg[:, tb, :], axis=AX,
                                     apply_absolute_value=True), V)
                wt(G, "mV", cnt["mV"])
                gd = G.dma_start(out=hs_d.rearrange("(tb p) (hc c) -> tb p hc c", p=128, c=512)[:, :, ehc, :].rearrange("tb p c -> p tb c"),
                                 in_=hstg[:])
                hs_dg[ehc] = dinc(gd, "g2")
            tmg = [0] * NQT
            for q in range(4):
                wt(T, "mV", qg_q[q])
                for f4 in range(4):
                    fc = 4 * q + f4
                    for tb in range(NQT):
                        if fc == 0 and hc >= 1:
                            wt(T, "mV", ev_tb[tb])
                        mm = T.matmul(out=bank(tb), lhsT=xqfm[:, fc, tb * 128:(tb + 1) * 128],
                                      rhs=tg[:, fc, :], start=(fc == 0), stop=(fc == FD - 1))
                        if fc == FD - 1:
                            tmg[tb] = inc(mm, T)
            for tb in range(NQT):
                wt(S, "mT", tmg[tb])
                silu_sm[tb] = inc(S.activation(out=hg[:, tb, :], in_=bank(tb), func=AF.Silu,
                                               scale=sgs[:, tb:tb + 1]), S)
            tmu = [0] * NQT
            for q in range(4):
                wt(T, "mV", qu_q[q])
                for f4 in range(4):
                    fc = 4 * q + f4
                    for tb in range(NQT):
                        if fc == 0:
                            wt(T, "mS", silu_sm[tb])
                        mm = T.matmul(out=bank(tb), lhsT=xqfm[:, fc, tb * 128:(tb + 1) * 128],
                                      rhs=tu[:, fc, :], start=(fc == 0), stop=(fc == FD - 1))
                        if fc == FD - 1:
                            tmu[tb] = inc(mm, T)
            tq_used[setq] = cnt["mT"]
            pend = (hc, list(tmu))
        ehc, etmu = pend
        for tb in range(NQT):
            wt(V, "mT", etmu[tb])
            if tb == 0:
                wt(V, "g2", hs_dg[ehc - 1])
            inc(V.tensor_tensor(out=hstg[:, tb, :], in0=bank(tb), in1=hg[:, tb, :],
                                op=ALU.mult), V)
            inc(V.reduce_max(out=hmaxa[:, ehc, tb:tb + 1], in_=hstg[:, tb, :], axis=AX,
                             apply_absolute_value=True), V)
        wt(G, "mV", cnt["mV"])
        gd = G.dma_start(out=hs_d.rearrange("(tb p) (hc c) -> tb p hc c", p=128, c=512)[:, :, ehc, :].rearrange("tb p c -> p tb c"),
                         in_=hstg[:])
        hs_dg[ehc] = dinc(gd, "g2")
        for b_ in (4, 5, 6, 7):
            tb_ev[b_] = cnt["mV"]

        maybe_stop('P2a')
        s_last = [0] * NQT
        act_q = [0, 0, 0, 0]
        dqd = [0, 0, 0, 0]
        QS = ("w0", "w1", "x0", "x1")
        for vv in sorted(sq_done.values()):
            wt(SY, "mV", vv)
        for tb in range(NQT):
            for qt in range(4):
                if tb == 0:
                    wt(SY, "g2", hs_dg[4 * qt + 3])
                if tb >= 1:
                    wt(SY, "mS", act_q[qt])
                dqd[qt] = dinc(SY.dma_start(out=hbuf[:, qt * D:(qt + 1) * D],
                                            in_=hs_d[tb * 128:(tb + 1) * 128, qt * D:(qt + 1) * D]),
                               QS[qt])
            if tb >= 1:
                wt(V, "mS", s_last[tb - 1])
            inc(V.reduce_max(out=hmax[:, tb:tb + 1],
                             in_=hmaxa[:, :, tb:tb + 1].rearrange("p f c -> p (f c)"),
                             axis=AX), V)
            inc(V.tensor_scalar(out=auxs[:, 8:9], in0=hmax[:, tb:tb + 1],
                                scalar1=1.0 / 127.0, scalar2=None, op0=ALU.mult), V)
            vm = inc(V.reciprocal(out=auxs[:, 9:10], in_=auxs[:, 8:9]), V)
            wt(S, "mV", vm)
            for qt in range(4):
                sl = slice(qt * D, (qt + 1) * D)
                wt(S, QS[qt], dqd[qt])
                a = S.activation(out=q2_u[:], in_=hbuf[:, sl], func=AF.Identity,
                                 scale=auxs[:, 9:10], bias=F16B)
                sm = inc(a, S)
                act_q[qt] = sm
                s_last[tb] = sm
                wt(V, "mS", sm)
                inc(V.tensor_copy(out=q2_h[:], in_=q2_u[:]), V)
                vm = inc(V.tensor_scalar(out=hqtm[:], in0=q2_h[:], scalar1=-F16B,
                                         scalar2=None, op0=ALU.add), V)
                wt(T, "mV", vm)
                for i4 in range(4):
                    b = 4 + i4
                    wt(T, "mV", tb_ev[b])
                    for sl4 in range(4):
                        i2 = i4 * 4 + sl4
                        mm = T.matmul(out=bank(b)[:, sl4 * 128:(sl4 + 1) * 128],
                                      lhsT=hqtm[:, i2 * 128:(i2 + 1) * 128],
                                      rhs=identb[:], start=True, stop=True)
                    tm = inc(mm, T)
                    wt(V, "mT", tm)
                    ev = V.tensor_copy(
                        out=hqfm[:, qt * FD + i4 * 4:qt * FD + i4 * 4 + 4, tb * 128:(tb + 1) * 128],
                        in_=bank(b)[:].rearrange("p (f c) -> p f c", c=128))
                    tb_ev[b] = inc(ev, V)
                wt(S, "mV", vm)
        inc(V.tensor_tensor(out=dscl[:], in0=hmax[:], in1=xnmax[:], op=ALU.mult), V)
        inc(V.tensor_scalar(out=dscl[:], in0=dscl[:], scalar1=thr[:, 7:8],
                            scalar2=None, op0=ALU.mult), V)
        p2b_V = cnt["mV"]
        p2b_S = cnt["mS"]

        maybe_stop('P2b')
        dsrc = wd_d.rearrange("(hc p) (oc c) -> hc p oc c", p=128, c=128)
        wt(SY, "mS", p2b_S)
        wt(V, "mS", p2b_S)
        ev_bank[4] = p2b_V
        tmq_prev = [0, 0, 0, 0]
        e_done = {0: 0, 1: 0}
        pend_c = None
        for oc in range(FD):
            qd_q = [0] * 4
            for ei in range(8):
                slab = dslab[:, 0:8, :] if ei % 2 == 0 else dslab[:, 8:16, :]
                ssem = "w0" if ei % 2 == 0 else "w1"
                if e_done[ei % 2]:
                    wt(SY, "mV", e_done[ei % 2])
                dd = dinc(SY.dma_start(out=slab,
                                       in_=dsrc[ei * 8:(ei + 1) * 8, :, oc, :].rearrange("f p c -> p f c")), ssem)
                fl = slab.rearrange("p f c -> p (f c)")
                fa = dcma[:, 0:8, :].rearrange("p f c -> p (f c)")
                fq = tdq[:, ei * 8:(ei + 1) * 8, :].rearrange("p f c -> p (f c)")
                wt(V, ssem, dd)
                if ei % 2 == 0:
                    wt(V, "mT", tmq_prev[ei // 2])
                inc(V.tensor_scalar(out=fa, in0=fl, scalar1=thr[:, 5:6],
                                    scalar2=None, op0=ALU.is_lt), V)
                e_done[ei % 2] = inc(V.scalar_tensor_tensor(
                    out=fq, in0=fl, scalar=thr[:, 4:5], in1=fa,
                    op0=ALU.is_gt, op1=ALU.subtract), V)
                if ei % 2 == 1:
                    qd_q[ei // 2] = e_done[1]
            if pend_c is not None:
                eoc, ebp, etm = pend_c
                wt(V, "mT", etm)
                ev = V.tensor_copy(out=ffnb[:, eoc, :], in_=bank2(ebp))
                ev_bank[ebp] = inc(ev, V)
            bp = 2 if oc % 2 == 0 else 4
            for qt in range(4):
                wt(T, "mV", qd_q[qt])
                if qt == 0:
                    wt(T, "mV", ev_bank.get(bp, 0))
                for nh in range(2):
                    for hc16 in range(FD):
                        hcc = qt * FD + hc16
                        mm = T.matmul(out=bank(bp + nh), lhsT=tdq[:, hcc, :],
                                      rhs=hqfm[:, hcc, nh * 512:(nh + 1) * 512],
                                      start=(hcc == 0), stop=(hcc == 63))
                tmq_prev[qt] = inc(mm, T)
            pend_c = (oc, bp, tmq_prev[3])
        eoc, ebp, etm = pend_c
        wt(V, "mT", etm)
        ev = V.tensor_copy(out=ffnb[:, eoc, :], in_=bank2(ebp))
        ev_bank[ebp] = inc(ev, V)

        maybe_stop('P2c')
        wt(G, "mT", tmq_prev[0])
        wt(G, "g0", aos_dg[FD - 2])
        wt(G, "g1", aos_dg[FD - 1])
        ao_dd = dinc(G.dma_start(out=aofm[:], in_=aos_d.rearrange("f p c -> p f c")), "g2")
        wt(SY, "mT", tmq_prev[2])
        wt(T, "g2", ao_dd)
        wt(T, "mV", cnt["mV"])
        y_ev = [0] * NQT
        y_dg = [0] * NQT
        s_cp = [0] * NQT
        v_stt = [0] * NQT
        for tb in range(NQT):
            buf = xtb[:, tb % 2, :]
            if tb >= 2:
                wt(SY, "mV", y_ev[tb - 2])
            dd = dinc(SY.dma_start(out=buf, in_=xo_d[tb * 128:(tb + 1) * 128, :]),
                      "x0" if tb % 2 == 0 else "x1")
            if tb >= 1:
                wt(T, "mS", s_cp[tb - 1])
            for oc in range(FD):
                bt = (psb0, psb1, psb6, psb7)[oc // 4]
                tm_ = T.matmul(out=bt[:, (oc % 4) * 128:(oc % 4) * 128 + 128],
                               lhsT=aofm[:, oc, tb * 128:(tb + 1) * 128],
                               rhs=identb[:], start=True, stop=True)
            tm = inc(tm_, T)
            if tb >= 1:
                wt(T, "mV", v_stt[tb - 1])
            for oc in range(FD):
                pt = psp23 if oc < 8 else psp45
                tm2_ = T.matmul(out=pt[:, (oc % 8) * 128:(oc % 8) * 128 + 128],
                                lhsT=ffnb[:, oc, tb * 128:(tb + 1) * 128],
                                rhs=identb[:], start=True, stop=True)
            tm2 = inc(tm2_, T)
            ys = ystg[:, tb % 2, :]
            wt(S, "mT", tm)
            if tb >= 2:
                wt(S, "g0" if tb % 2 == 0 else "g1", y_dg[tb - 2])
            inc(S.activation(out=ys[:, 0:512], in_=psb0[:], func=AF.Copy), S)
            inc(S.activation(out=ys[:, 512:1024], in_=psb1[:], func=AF.Copy), S)
            inc(S.activation(out=ys[:, 1024:1536], in_=psb6[:], func=AF.Copy), S)
            sm4 = inc(S.activation(out=ys[:, 1536:2048], in_=psb7[:], func=AF.Copy), S)
            s_cp[tb] = sm4
            wt(V, "mS", sm4)
            wt(V, "mT", tm2)
            wt(V, "x0" if tb % 2 == 0 else "x1", dd)
            inc(V.scalar_tensor_tensor(out=ys[:, 0:1024], in0=psp23[:],
                                       scalar=dscl[:, tb:tb + 1], in1=ys[:, 0:1024],
                                       op0=ALU.mult, op1=ALU.add), V)
            v_stt[tb] = inc(V.scalar_tensor_tensor(out=ys[:, 1024:2048], in0=psp45[:],
                                                   scalar=dscl[:, tb:tb + 1], in1=ys[:, 1024:2048],
                                                   op0=ALU.mult, op1=ALU.add), V)
            ev = V.tensor_tensor(out=ys, in0=ys, in1=buf, op=ALU.add)
            y_ev[tb] = inc(ev, V)
            wt(G, "mV", y_ev[tb])
            gd = G.dma_start(out=y_d[tb * 128:(tb + 1) * 128, :], in_=ystg[:, tb % 2, :])
            y_dg[tb] = dinc(gd, "g0" if tb % 2 == 0 else "g1")
        wt(G, "g0", cnt["g0"])
        wt(G, "g1", cnt["g1"])
        wt(SY, "g0", cnt["g0"])
        wt(SY, "g1", cnt["g1"])
        return cnt

    with nc.Block() as block:
        counts = {}

        def _sect(name):
            cnt = {k: 0 for k in ("c0", "x0", "x1", "w0", "w1", "g0", "g1", "g2", "p0", "p1",
                                  "mT", "mV", "mS", "mG", "mY")}
            try:
                counts[name] = emit(nc, name, cnt)
            except Exception as ex:
                if type(ex).__name__ == "_EarlyStop":
                    counts[name] = cnt
                else:
                    raise

        @block.sync
        def _(e):
            _sect("sync")

        @block.vector
        def _(e):
            _sect("vector")

        @block.scalar
        def _(e):
            _sect("scalar")

        @block.tensor
        def _(e):
            _sect("tensor")

        @block.gpsimd
        def _(e):
            _sect("gpsimd")

    base = counts["sync"]
    for k2, v2 in counts.items():
        assert v2 == base, (k2, v2, base)
    for cm in sem_cms:
        cm.__exit__(None, None, None)
    return nc


_BF = ml_dtypes.bfloat16


def _consts():
    masks = np.zeros((512, 512), dtype=_BF)
    for p in range(4):
        kk = np.arange(128)[:, None]
        qq = np.arange(512)[None, :]
        masks[p * 128:(p + 1) * 128, :] = (qq >= kk + p * 128).astype(_BF)
    ones = np.ones((128, 128), dtype=_BF)
    ident = np.eye(128, dtype=np.float32).astype(_BF)
    return masks, ones, ident


def kernel(L0_attn_x, L0_attn_w_dq, L0_attn_w_uq, L0_attn_w_dkv, L0_attn_w_ukv,
           L0_attn_w_out, L0_ffn_w_gate, L0_ffn_w_up, L0_ffn_w_down):
    x = np.asarray(L0_attn_x, dtype=np.float32)
    wdqT = np.ascontiguousarray(np.asarray(L0_attn_w_dq, np.float32).T)
    wuqT = np.ascontiguousarray(np.asarray(L0_attn_w_uq, np.float32).T)
    wdkvT = np.ascontiguousarray(np.asarray(L0_attn_w_dkv, np.float32).T)
    wukvT = np.ascontiguousarray(np.asarray(L0_attn_w_ukv, np.float32).T)
    woutT = np.ascontiguousarray(np.asarray(L0_attn_w_out, np.float32).T)
    wgT = np.ascontiguousarray(np.asarray(L0_ffn_w_gate, np.float32).T)
    wuT = np.ascontiguousarray(np.asarray(L0_ffn_w_up, np.float32).T)
    wdT = np.ascontiguousarray(np.asarray(L0_ffn_w_down, np.float32).T)
    masks, ones, ident = _consts()

    in_maps = []
    for c in range(NCORES):
        KA = 512 * (c + 1)
        KBn = NK - KA
        xk = np.ascontiguousarray(np.concatenate([x[0, :KA], x[1, :KBn]], axis=0))
        xo = np.ascontiguousarray(np.concatenate([x[0, KA - 512:KA], x[1, KBn - 512:KBn]], axis=0))
        in_maps.append(dict(xk=xk, xo=xo, wdqT=wdqT, wuqT=wuqT, wdkvT=wdkvT,
                            wukvT=wukvT, woutT=woutT, wgT=wgT, wuT=wuT, wdT=wdT,
                            masks=masks, onesc=ones, identc=ident,
                            cidc=np.full((1, 4), c, np.int32)))

    nc = build_nc()
    res = run_bass_kernel_spmd(nc, in_maps, core_ids=list(range(NCORES)))

    out = np.empty((2, 4096, D), dtype=np.float32)
    for c in range(NCORES):
        y = np.asarray(res.results[c]["y"], dtype=np.float32)
        KA = 512 * (c + 1)
        KBn = NK - KA
        out[0, KA - 512:KA] = y[0:512]
        out[1, KBn - 512:KBn] = y[512:1024]
    return out
